# version 14
# baseline (speedup 1.0000x reference)
import math
import numpy as np
import ml_dtypes
from contextlib import ExitStack

import concourse.bass as bass
import concourse.mybir as mybir
from concourse.bass_utils import run_bass_kernel_spmd

F32 = mybir.dt.float32
BF16 = mybir.dt.bfloat16
I32 = mybir.dt.int32
AF = mybir.ActivationFunctionType
ALU = mybir.AluOpType
AX = mybir.AxisListType

D = 2048
NCH = 16
TOWN = 2048
HALO = 256
TK = TOWN + HALO
G = 16
TKG = TK + G
NH = 16
HD = 128
NG = 8
NQT = 16
NIT = 32
I_FF = 8192
NIC = 64
EPS = 1e-5
QSCALE = 1.0 / math.sqrt(HD)
NEG = -1e9
P1_SLABS = [(0, 512), (512, 512), (1024, 512), (1536, 512), (2048, 256)]


class SemC:

    def __init__(self, handle):
        self.h = handle
        self.n = 0

    def inc(self, instr, k=1):
        instr.then_inc(self.h, k)
        self.n += k
        return instr

    def wait(self, eng, val):
        if val > 0:
            eng.wait_ge(self.h, val)


class Builder:
    def __init__(self, stage=4):
        self.stage = stage
        self.nc = bass.Bass("TRN2", target_bir_lowering=False,
                            detect_race_conditions=False)

    def declare_dram(self):
        nc, stage = self.nc, self.stage
        dbg = stage < 4
        EXT, INT = "ExternalOutput", "Internal"
        di = lambda n, s, d: nc.dram_tensor(n, s, d, kind="ExternalInput").ap()
        self.xT = di("xT", [128, NCH, TK], F32)
        self.gT = di("gT", [128, NCH, G], F32)
        self.wq = di("wq", [128, NCH, D], F32)
        self.wk = di("wk", [128, NCH, D], F32)
        self.wv = di("wv", [128, NCH, D], F32)
        self.wo = di("wo", [128, NCH, D], F32)
        self.gate = di("gate", [128, NCH, I_FF], I32)
        self.upw = di("upw", [128, NCH, I_FF], I32)
        self.down = di("down", [NCH, 128, NIC, 128], I32)
        self.masks = di("masks", [128, 3, 400], F32)
        self.gamma = di("gamma", [128, NCH], F32)
        self.beta = di("beta", [128, NCH], F32)
        self.gsc = di("gsc", [128, NIC], F32)
        self.usc = di("usc", [128, NIC], F32)
        self.dsc = di("dsc", [128, NCH], F32)
        self.identin = di("identin", [128, 128], BF16)

        self.outT = nc.dram_tensor("outT", [128, NCH, TOWN], F32,
                                   kind=(EXT if stage >= 4 else INT)).ap()
        self.xnsp = nc.dram_tensor("xnsp", [128, NCH, TOWN], BF16,
                                   kind=(EXT if dbg else INT)).ap()
        self.osp = nc.dram_tensor("osp", [128, NH, TOWN], BF16,
                                  kind=(EXT if (dbg and stage >= 2) else INT)).ap()
        self.ssp = nc.dram_tensor("ssp", [128, NCH, TOWN], F32,
                                  kind=(EXT if (dbg and stage >= 3) else INT)).ap()
        self.gbf = nc.dram_tensor("gbf", [128, NCH, I_FF], BF16, kind=INT).ap()
        self.ubf = nc.dram_tensor("ubf", [128, NCH, I_FF], BF16, kind=INT).ap()
        self.dbf = nc.dram_tensor("dbf", [NCH, 128, NIC, 128], BF16, kind=INT).ap()
        if dbg:
            self.xn2sp = nc.dram_tensor("xn2sp", [128, NCH, TKG], BF16, kind=EXT).ap()
            self.rowsp = nc.dram_tensor("rowsp", [2, TK], F32, kind=EXT).ap()
        if dbg and stage == 2:
            self.qdbg = nc.dram_tensor("qdbg", [128, 2 * TOWN], BF16, kind=EXT).ap()
            self.kdbg = nc.dram_tensor("kdbg", [128, 2 * TKG], BF16, kind=EXT).ap()
            self.vdbg = nc.dram_tensor("vdbg", [128, 19 * 256], BF16, kind=EXT).ap()
            self.edbg = nc.dram_tensor("edbg", [128, 3 * 400], F32, kind=EXT).ap()
            self.ptdbg = nc.dram_tensor("ptdbg", [128, 512 + 400], BF16, kind=EXT).ap()
            self.rdbg = nc.dram_tensor("rdbg", [128, 3], F32, kind=EXT).ap()

    def sbt(self, stack, name, shape, dt=F32):
        return stack.enter_context(self.nc.sbuf_tensor(name, shape, dt))

    def pst(self, stack, name, shape, dt=F32):
        return stack.enter_context(self.nc.psum_tensor(name, shape, dt))

    def sem(self, stack, name):
        return SemC(stack.enter_context(self.nc.semaphore(name)))

    def barrier(self):
        self.phase += 1
        tgt = 5 * self.phase
        bar = self.bar

        def mk(engname):
            def _f(eng):
                eng.sem_inc(bar.h, 1)
                eng.wait_ge(bar.h, tgt)
            return _f
        for name in ("gpsimd", "scalar", "tensor", "vector", "sync"):
            getattr(self.blk, name)(mk(name))
        bar.n += 5

    def build(self):
        self.declare_dram()
        with ExitStack() as ctx:
            self.ctx = ctx
            nc = self.nc
            self.mask_sb = self.sbt(ctx, "mask_sb", [128, 3 * 400])
            self.gamma_sb = self.sbt(ctx, "gamma_sb", [128, NCH])
            self.beta_sb = self.sbt(ctx, "beta_sb", [128, NCH])
            self.gsc_sb = self.sbt(ctx, "gsc_sb", [128, NIC])
            self.usc_sb = self.sbt(ctx, "usc_sb", [128, NIC])
            self.dsc_sb = self.sbt(ctx, "dsc_sb", [128, NCH])
            self.ident = self.sbt(ctx, "ident", [128, 128], BF16)
            self.ones_f = self.sbt(ctx, "ones_f", [128, 1])
            self.onesrow_f = self.sbt(ctx, "onesrow_f", [1, 128])
            self.qscale_t = self.sbt(ctx, "qscale_t", [128, 1])
            self.eps_t = self.sbt(ctx, "eps_t", [1, 1])
            self.neghalf_t = self.sbt(ctx, "neghalf_t", [1, 1])
            self.rstd_all = self.sbt(ctx, "rstd_all", [1, TK])
            self.d2_all = self.sbt(ctx, "d2_all", [1, TK])

            self.bar = self.sem(ctx, "bar")
            self.bg = self.sem(ctx, "bg")
            self.wl = self.sem(ctx, "wl")
            self.st_o = [self.sem(ctx, "st_o0"), self.sem(ctx, "st_o1")]
            self.st_s = self.sem(ctx, "st_s")
            self.phase = 0

            with nc.Block("main") as blk:
                self.blk = blk
                with ExitStack() as p12:
                    self.xn2T = self.sbt(p12, "xn2T", [128, NCH * TKG], BF16)
                    self.emit_p0()
                    self.barrier()
                    self.emit_p1()
                    self.barrier()
                    if self.stage >= 2:
                        with ExitStack() as p2s:
                            self.emit_p2(p2s)
                        self.barrier()
                if self.stage >= 3:
                    with ExitStack() as p2cs:
                        self.emit_p2c(p2cs)
                    self.barrier()
                if self.stage >= 4:
                    with ExitStack() as p3s:
                        self.emit_p3(p3s)
                self.emit_final()
        return self.nc

    def emit_p0(self):
        ctx = self.ctx
        self.d_p0 = self.sem(ctx, "d_p0")
        self.m_p0 = self.sem(ctx, "m_p0")
        d_p0, m_p0 = self.d_p0, self.m_p0
        b = self

        def p0(g):
            for dst, src in ((b.mask_sb, b.masks), (b.gamma_sb, b.gamma),
                             (b.beta_sb, b.beta), (b.gsc_sb, b.gsc),
                             (b.usc_sb, b.usc), (b.dsc_sb, b.dsc),
                             (b.ident, b.identin)):
                if len(src.shape) == 3:
                    d_p0.inc(g.dma_start(
                        dst[:].rearrange("p (a b) -> p a b", b=src.shape[2]), src), 16)
                else:
                    d_p0.inc(g.dma_start(dst[:], src), 16)
            d_p0.inc(g.dma_start(
                b.xn2T[:].rearrange("p (c t) -> p c t", t=TKG)[:, :, TK:TKG],
                b.gT), 16)
            m_p0.inc(g.memset(b.ones_f[:], 1.0))
            m_p0.inc(g.memset(b.onesrow_f[:], 1.0))
            m_p0.inc(g.memset(b.qscale_t[:], QSCALE))
            m_p0.inc(g.memset(b.eps_t[:], EPS))
            m_p0.inc(g.memset(b.neghalf_t[:], -0.5))
            g.wait_ge(d_p0.h, d_p0.n)
            g.wait_ge(m_p0.h, m_p0.n)
        self.blk.gpsimd(p0)

    def emit_p1(self):
        b = self
        with ExitStack() as p1:
            x_sb = [b.sbt(p1, "x_sb0", [128, NCH * 512]),
                    b.sbt(p1, "x_sb1", [128, NCH * 512])]
            xn_st = b.sbt(p1, "xn_st", [128, NCH * 512], BF16)
            sq_sb = b.sbt(p1, "sq_sb", [128, 2 * 512])
            bcast = b.sbt(p1, "bcast", [128, 3 * 512])
            s1_sb = b.sbt(p1, "s1_sb", [128, 512])
            s2_sb = b.sbt(p1, "s2_sb", [128, 512])
            r_mu = b.sbt(p1, "r_mu", [1, 512])
            r_msq = b.sbt(p1, "r_msq", [1, 512])
            r_tmp = b.sbt(p1, "r_tmp", [1, 512])
            r_var = b.sbt(p1, "r_var", [1, 512])
            r_rec = b.sbt(p1, "r_rec", [1, 512])
            r_v2 = b.sbt(p1, "r_v2", [1, 512])
            r_rstd2 = b.sbt(p1, "r_rstd2", [1, 512])
            ps_sum = b.pst(p1, "ps_sum", [1, 512])
            ps_sq = b.pst(p1, "ps_sq", [1, 512])
            ps_bc = [b.pst(p1, "ps_bc0", [128, 512]), b.pst(p1, "ps_bc1", [128, 512])]

            d_x1 = b.sem(p1, "d_x1")
            d_xn = b.sem(p1, "d_xn")
            pe_sum = b.sem(p1, "pe_sum")
            pe_sq = b.sem(p1, "pe_sq")
            pe_bc = b.sem(p1, "pe_bc")
            act_sq = b.sem(p1, "act_sq")
            act_row = b.sem(p1, "act_row")
            act_bc = b.sem(p1, "act_bc")
            dve_pre = b.sem(p1, "dve_pre")
            dve_stat = b.sem(p1, "dve_stat")
            dve_row = b.sem(p1, "dve_row")
            dve_chunk = b.sem(p1, "dve_chunk")

            def p1_sync(sy):
                def load(s):
                    t0, W = P1_SLABS[s]
                    d_x1.inc(sy.dma_start(
                        x_sb[s % 2][:, 0:NCH * W].rearrange("p (c t) -> p c t", t=W),
                        b.xT[:, :, t0:t0 + W]), 16)
                load(0)
                load(1)
                for s, (t0, W) in enumerate(P1_SLABS):
                    dve_chunk.wait(sy, 16 * (s + 1))
                    o0 = max(t0, HALO)
                    so = o0 - t0
                    d_xn.inc(sy.dma_start(
                        b.xnsp[:, :, o0 - HALO:t0 + W - HALO],
                        xn_st[:, 0:NCH * W].rearrange("p (c t) -> p c t", t=W)
                        [:, :, so:W]), 16)
                    if s + 2 < len(P1_SLABS):
                        act_sq.wait(sy, 16 * (s + 1))
                        pe_sum.wait(sy, s + 1)
                        load(s + 2)
                sy.wait_ge(d_xn.h, d_xn.n)
            b.blk.sync(p1_sync)

            def p1_tensor(t):
                for s, (t0, W) in enumerate(P1_SLABS):
                    d_x1.wait(t, 16 * (s + 1))
                    dve_row.wait(t, s)
                    for c in range(NCH):
                        mm = t.matmul(ps_sum[:, 0:W], b.ones_f[:],
                                      x_sb[s % 2][:, c * W:(c + 1) * W],
                                      start=(c == 0), stop=(c == NCH - 1))
                    pe_sum.inc(mm)
                    for c in range(NCH):
                        act_sq.wait(t, 16 * s + c + 1)
                        mm = t.matmul(ps_sq[:, 0:W], b.ones_f[:],
                                      sq_sb[:, (c % 2) * 512:(c % 2) * 512 + W],
                                      start=(c == 0), stop=(c == NCH - 1))
                        pe_sq.inc(mm)
                    act_row.wait(t, 2 * (s + 1))
                    dve_stat.wait(t, s + 1)
                    rows = (b.rstd_all[0:1, t0:t0 + W], b.d2_all[0:1, t0:t0 + W],
                            r_rstd2[0:1, 0:W])
                    for j, row in enumerate(rows):
                        act_bc.wait(t, 3 * s + j - 1)
                        pe_bc.inc(t.matmul(ps_bc[j % 2][:, 0:W], b.onesrow_f[:],
                                           row, start=True, stop=True))
            b.blk.tensor(p1_tensor)

            def p1_scalar(a):
                for s, (t0, W) in enumerate(P1_SLABS):
                    d_x1.wait(a, 16 * (s + 1))
                    for c in range(NCH):
                        pe_sq.wait(a, 16 * s + c - 1)
                        act_sq.inc(a.activation(
                            sq_sb[:, (c % 2) * 512:(c % 2) * 512 + W],
                            x_sb[s % 2][:, c * W:(c + 1) * W], AF.Square))
                    dve_pre.wait(a, s + 1)
                    a.activation(r_tmp[0:1, 0:W], r_var[0:1, 0:W], AF.Ln,
                                 bias=b.eps_t[:])
                    act_row.inc(a.activation(b.rstd_all[0:1, t0:t0 + W],
                                             r_tmp[0:1, 0:W], AF.Exp,
                                             scale=b.neghalf_t[:]))
                    a.activation(r_tmp[0:1, 0:W], r_v2[0:1, 0:W], AF.Ln,
                                 bias=b.eps_t[:])
                    act_row.inc(a.activation(r_rstd2[0:1, 0:W], r_tmp[0:1, 0:W],
                                             AF.Exp, scale=b.neghalf_t[:]))
                    dve_chunk.wait(a, 16 * s)
                    for j in range(3):
                        pe_bc.wait(a, 3 * s + j + 1)
                        act_bc.inc(a.activation(bcast[:, j * 512:j * 512 + W],
                                                ps_bc[j % 2][:, 0:W], AF.Copy))
            b.blk.scalar(p1_scalar)

            def p1_vector(v):
                for s, (t0, W) in enumerate(P1_SLABS):
                    pe_sum.wait(v, s + 1)
                    pe_sq.wait(v, 16 * (s + 1))
                    act_row.wait(v, 2 * s)
                    v.tensor_scalar(r_mu[0:1, 0:W], ps_sum[0:1, 0:W],
                                    1.0 / D, None, ALU.mult)
                    dve_row.inc(v.tensor_scalar(r_msq[0:1, 0:W], ps_sq[0:1, 0:W],
                                                1.0 / D, None, ALU.mult))
                    v.tensor_mul(r_tmp[0:1, 0:W], r_mu[0:1, 0:W], r_mu[0:1, 0:W])
                    v.tensor_sub(r_var[0:1, 0:W], r_msq[0:1, 0:W], r_tmp[0:1, 0:W])
                    v.tensor_scalar(r_rec[0:1, 0:W], r_var[0:1, 0:W],
                                    EPS, None, ALU.add)
                    v.reciprocal(r_rec[0:1, 0:W], r_rec[0:1, 0:W])
                    dve_pre.inc(v.tensor_mul(r_v2[0:1, 0:W], r_var[0:1, 0:W],
                                             r_rec[0:1, 0:W]))
                    act_row.wait(v, 2 * (s + 1))
                    dve_stat.inc(v.scalar_tensor_tensor(
                        b.d2_all[0:1, t0:t0 + W], r_mu[0:1, 0:W], -1.0,
                        b.rstd_all[0:1, t0:t0 + W], ALU.mult, ALU.mult))
                    act_bc.wait(v, 3 * (s + 1))
                    d_xn.wait(v, 16 * s)
                    for c in range(NCH):
                        xc = x_sb[s % 2][:, c * W:(c + 1) * W]
                        v.tensor_mul(s1_sb[:, 0:W], xc, bcast[:, 0:W])
                        v.tensor_add(s2_sb[:, 0:W], s1_sb[:, 0:W],
                                     bcast[:, 512:512 + W])
                        v.tensor_copy(xn_st[:, c * W:(c + 1) * W], s2_sb[:, 0:W])
                        v.tensor_mul(s1_sb[:, 0:W], s2_sb[:, 0:W],
                                     bcast[:, 2 * 512:2 * 512 + W])
                        dve_chunk.inc(v.tensor_scalar(
                            b.xn2T[:, c * TKG + t0:c * TKG + t0 + W],
                            s1_sb[:, 0:W], b.gamma_sb[:, c:c + 1],
                            b.beta_sb[:, c:c + 1], ALU.mult, ALU.add))
            b.blk.vector(p1_vector)

            if b.stage == 1:
                d_dbg = b.sem(p1, "d_dbg")

                def p1_dbg(sy):
                    dve_chunk.wait(sy, 16 * len(P1_SLABS))
                    d_dbg.inc(sy.dma_start(
                        b.xn2sp, b.xn2T[:].rearrange("p (c t) -> p c t", t=TKG)), 16)
                    d_dbg.inc(sy.dma_start(b.rowsp[0:1, :], b.rstd_all[:]), 16)
                    d_dbg.inc(sy.dma_start(b.rowsp[1:2, :], b.d2_all[:]), 16)
                    sy.wait_ge(d_dbg.h, d_dbg.n)
                b.blk.sync(p1_dbg)

    def emit_p2(self, p2):
        b = self
        b.wq_sl = b.sbt(p2, "wq_sl", [128, NCH * 256], BF16)
        b.wk_sl = b.sbt(p2, "wk_sl", [128, NCH * 256], BF16)
        b.wv_sl = b.sbt(p2, "wv_sl", [128, NCH * 256], BF16)
        Q_sb = [b.sbt(p2, "Q_sb0", [128, 2 * TOWN], BF16),
                b.sbt(p2, "Q_sb1", [128, 2 * TOWN], BF16)]
        K_sb = [b.sbt(p2, "K_sb0", [128, 2 * TKG], BF16),
                b.sbt(p2, "K_sb1", [128, 2 * TKG], BF16)]
        V_sb = [b.sbt(p2, "V_sb0", [128, 19 * 256], BF16),
                b.sbt(p2, "V_sb1", [128, 19 * 256], BF16)]
        O_sb = [b.sbt(p2, "O_sb0", [128, TOWN], BF16),
                b.sbt(p2, "O_sb1", [128, TOWN], BF16)]
        Z_sb = [b.sbt(p2, "Z_sb0", [128, 400]), b.sbt(p2, "Z_sb1", [128, 400])]
        E_sb = [b.sbt(p2, "E_sb0", [128, 400], BF16),
                b.sbt(p2, "E_sb1", [128, 400], BF16)]
        PT_sb = [b.sbt(p2, "PT_sb0", [128, 512], BF16),
                 b.sbt(p2, "PT_sb1", [128, 512], BF16)]
        dg_sb = [b.sbt(p2, "dg_sb0", [128, 128], BF16),
                 b.sbt(p2, "dg_sb1", [128, 128], BF16)]
        nm_sb = [b.sbt(p2, "nm_sb0", [128, 1]), b.sbt(p2, "nm_sb1", [128, 1])]
        l_sb = [b.sbt(p2, "l_sb0", [128, 1]), b.sbt(p2, "l_sb1", [128, 1])]
        li_sb = [b.sbt(p2, "li_sb0", [128, 1]), b.sbt(p2, "li_sb1", [128, 1])]
        ps_pr = [b.pst(p2, "ps_pr0", [128, 512]), b.pst(p2, "ps_pr1", [128, 512])]
        ps_S = [b.pst(p2, "ps_S0", [128, 512]), b.pst(p2, "ps_S1", [128, 512])]
        ps_PT = [b.pst(p2, "ps_PT0", [128, 512]), b.pst(p2, "ps_PT1", [128, 512])]
        ps_O = [b.pst(p2, "ps_O0", [128, 512]), b.pst(p2, "ps_O1", [128, 512])]

        pe_prod = b.sem(p2, "pe_prod")
        act_prod = b.sem(p2, "act_prod")
        pe_s = b.sem(p2, "pe_s")
        pe_t = b.sem(p2, "pe_t")
        pe_av = b.sem(p2, "pe_av")
        dve_z = b.sem(p2, "dve_z")
        dve_l = b.sem(p2, "dve_l")
        dve_p = b.sem(p2, "dve_p")
        act_e = b.sem(p2, "act_e")
        act_pt = b.sem(p2, "act_pt")
        act_o = b.sem(p2, "act_o")
        NPR = 37

        def prod_due(i):
            return min(NPR, ((i + 1) * NPR + NIT + 1) // (NIT + 2))

        NBG = 96
        NBGG = NBG // NG
        bg_st = [b.sbt(p2, "bg_st0", [128, 4096], BF16),
                 b.sbt(p2, "bg_st1", [128, 4096], BF16)]
        d_bgin = b.sem(p2, "d_bgin")
        p_bg = b.sem(p2, "p_bg")

        def bg_srcdst(k):
            if k < 64:
                ib, r = k // 4, k % 4
                j = 2 * ib + (r % 2)
                src, dst = (b.gate, b.gbf) if r < 2 else (b.upw, b.ubf)
                return (src[:, :, j * 256:(j + 1) * 256],
                        dst[:, :, j * 256:(j + 1) * 256], 256)
            kk = k - 64
            hc, h = kk // 2, kk % 2
            return (b.down[hc, :, h * 32:(h + 1) * 32, :],
                    b.dbf[hc, :, h * 32:(h + 1) * 32, :], 128)

        def bg_issue_in(g, k):
            src, _, inner = bg_srcdst(k)
            d_bgin.inc(g.dma_start(
                bg_st[k % 2][:].rearrange("p (a i) -> p a i", i=inner), src), 16)

        def bg_step(g, k):
            _, dst, inner = bg_srcdst(k)
            p_bg.wait(g, k + 1)
            b.bg.inc(g.dma_start(
                dst, bg_st[k % 2][:].rearrange("p (a i) -> p a i", i=inner)), 16)
            if k + 2 < NBG:
                g.wait_ge(b.bg.h, b.bg.n)
                bg_issue_in(g, k + 2)

        def slice_dma(g, gi, which):
            w, sl = ((b.wq, b.wq_sl), (b.wk, b.wk_sl), (b.wv, b.wv_sl))[which]
            b.wl.inc(g.dma_start(
                sl[:].rearrange("p (c o) -> p c o", o=256),
                w[:, :, gi * 256:(gi + 1) * 256]), 16)

        def p2_gpsimd(g):
            for which in range(3):
                slice_dma(g, 0, which)
            bg_issue_in(g, 0)
            bg_issue_in(g, 1)
            for which, gate in ((0, 8), (1, 18), (2, 37)):
                pe_prod.wait(g, gate)
                slice_dma(g, 1, which)
            for gi in range(NG):
                for idx, k in enumerate(range(NBGG * gi, NBGG * (gi + 1))):
                    bg_step(g, k)
                    if gi + 2 < NG:
                        if idx == 4:
                            pe_prod.wait(g, NPR * (gi + 1) + 8)
                            slice_dma(g, gi + 2, 0)
                        elif idx == 6:
                            pe_prod.wait(g, NPR * (gi + 1) + 18)
                            slice_dma(g, gi + 2, 1)
                if gi + 2 < NG:
                    pe_prod.wait(g, NPR * (gi + 2))
                    slice_dma(g, gi + 2, 2)
            g.wait_ge(b.bg.h, b.bg.n)
        b.blk.gpsimd(p2_gpsimd)

        def prod_pairs(gi):
            out = []
            e = gi % 2
            for hh in range(2):
                for sl in range(4):
                    pairs = [(b.wq_sl[:, c * 256 + hh * 128:c * 256 + (hh + 1) * 128],
                              b.xn2T[:, c * TKG + HALO + sl * 512:c * TKG + HALO + (sl + 1) * 512])
                             for c in range(NCH)]
                    out.append((512, pairs,
                                Q_sb[e][:, hh * TOWN + sl * 512:hh * TOWN + (sl + 1) * 512],
                                b.qscale_t))
            kslabs = [(0, 512), (512, 512), (1024, 512), (1536, 512), (2048, 272)]
            for hh in range(2):
                for (t0, W) in kslabs:
                    pairs = [(b.wk_sl[:, c * 256 + hh * 128:c * 256 + (hh + 1) * 128],
                              b.xn2T[:, c * TKG + t0:c * TKG + t0 + W])
                             for c in range(NCH)]
                    out.append((W, pairs,
                                K_sb[e][:, hh * TKG + t0:hh * TKG + t0 + W], None))
            for kt in range(18):
                pairs = [(b.xn2T[:, c * TKG + kt * 128:c * TKG + (kt + 1) * 128],
                          b.wv_sl[:, c * 256:(c + 1) * 256])
                         for c in range(NCH)]
                out.append((256, pairs, V_sb[e][:, kt * 256:(kt + 1) * 256], None))
            pairs = [(b.xn2T[:, c * TKG + TK:c * TKG + TKG],
                      b.wv_sl[:, c * 256:(c + 1) * 256])
                     for c in range(NCH)]
            out.append((256, pairs, V_sb[e][0:16, 18 * 256:19 * 256], None))
            return out

        def emit_prod(t, gi, j):
            gidx = NPR * gi + j
            if j == 0:
                b.wl.wait(t, 48 * gi + 16)
            elif j == 8:
                b.wl.wait(t, 48 * gi + 32)
            elif j == 18:
                b.wl.wait(t, 48 * gi + 48)
            act_prod.wait(t, gidx if j == 0 else gidx - 1)
            W, pairs, dst, scale = prod_groups[gi][j]
            for k, (lh, rh) in enumerate(pairs):
                mm = t.matmul(ps_pr[j % 2][0:dst.shape[0], 0:W], lh, rh,
                              start=(k == 0), stop=(k == len(pairs) - 1))
            pe_prod.inc(mm)

        def emit_evac(a, gi, j):
            gidx = NPR * gi + j
            pe_prod.wait(a, gidx + 1)
            W, pairs, dst, scale = prod_groups[gi][j]
            rows = dst.shape[0]
            if scale is not None:
                act_prod.inc(a.activation(dst, ps_pr[j % 2][0:rows, 0:W],
                                          AF.Copy, scale=scale[0:rows, :]))
            else:
                act_prod.inc(a.activation(dst, ps_pr[j % 2][0:rows, 0:W],
                                          AF.Copy))

        prod_groups = [prod_pairs(gi) for gi in range(NG)]

        def p2_tensor(t):
            for j in range(NPR):
                emit_prod(t, 0, j)
            for gi in range(NG):
                base = NIT * gi
                emitted = 0
                has_next = gi + 1 < NG
                for i in range(NIT + 2):
                    it = base + i
                    qt, hh = i // 2, i % 2
                    if i == 0:
                        act_prod.wait(t, NPR * gi + 18)
                    if i == 2:
                        act_prod.wait(t, NPR * (gi + 1))
                    if i < NIT:
                        dve_z.wait(t, it - 1)
                        lq = Q_sb[gi % 2][:, hh * TOWN + qt * 128:hh * TOWN + (qt + 1) * 128]
                        t.matmul(ps_S[it % 2][:, 0:384], lq,
                                 K_sb[gi % 2][:, hh * TKG + qt * 128:hh * TKG + qt * 128 + 384],
                                 start=True, stop=True)
                        pe_s.inc(t.matmul(ps_S[it % 2][:, 384:400], lq,
                                          K_sb[gi % 2][:, hh * TKG + TK:hh * TKG + TKG],
                                          start=True, stop=True))
                    if 0 <= i - 1 < NIT:
                        j = it - 1
                        act_e.wait(t, j + 1)
                        dve_p.wait(t, j + 1)
                        act_pt.wait(t, j - 1)
                        for k in range(3):
                            t.matmul(ps_PT[j % 2][:, k * 128:(k + 1) * 128],
                                     E_sb[j % 2][:, k * 128:(k + 1) * 128],
                                     dg_sb[j % 2][:], start=True, stop=True)
                        pe_t.inc(t.matmul(ps_PT[j % 2][0:16, 384:512],
                                          E_sb[j % 2][:, 384:400],
                                          dg_sb[j % 2][:], start=True, stop=True))
                    if 0 <= i - 2 < NIT:
                        j = it - 2
                        jq, jh = (i - 2) // 2, (i - 2) % 2
                        act_pt.wait(t, j + 1)
                        act_o.wait(t, j - 1)
                        for k in range(3):
                            kt = jq + k
                            t.matmul(ps_O[j % 2][:, 0:128],
                                     V_sb[gi % 2][:, kt * 256 + jh * 128:kt * 256 + (jh + 1) * 128],
                                     PT_sb[j % 2][:, k * 128:(k + 1) * 128],
                                     start=(k == 0), stop=False)
                        pe_av.inc(t.matmul(
                            ps_O[j % 2][:, 0:128],
                            V_sb[gi % 2][0:16, 18 * 256 + jh * 128:18 * 256 + (jh + 1) * 128],
                            PT_sb[j % 2][0:16, 384:512],
                            start=False, stop=True))
                    if has_next and (i % 2 == 1 or i >= NIT):
                        while emitted < prod_due(i):
                            emit_prod(t, gi + 1, emitted)
                            emitted += 1
        b.blk.tensor(p2_tensor)

        def p2_scalar(a):
            for j in range(NPR):
                emit_evac(a, 0, j)
            for gi in range(NG):
                base = NIT * gi
                emitted = 0
                has_next = gi + 1 < NG
                for i in range(NIT + 2):
                    it = base + i
                    if i < NIT:
                        dve_z.wait(a, it + 1)
                        pe_t.wait(a, it - 1)
                        dve_l.wait(a, it - 1)
                        act_e.inc(a.activation(E_sb[it % 2][:], Z_sb[it % 2][:],
                                               AF.Exp, bias=nm_sb[it % 2][:],
                                               accum_out=l_sb[it % 2][:]))
                    if 0 <= i - 1 < NIT:
                        j = it - 1
                        pe_t.wait(a, j + 1)
                        pe_av.wait(a, j - 1)
                        a.activation(PT_sb[j % 2][:, 0:384], ps_PT[j % 2][:, 0:384],
                                     AF.Copy)
                        act_pt.inc(a.activation(PT_sb[j % 2][0:16, 384:512],
                                                ps_PT[j % 2][0:16, 384:512], AF.Copy))
                    if 0 <= i - 2 < NIT:
                        j = it - 2
                        jq, jh = (i - 2) // 2, (i - 2) % 2
                        pe_av.wait(a, j + 1)
                        if jq == 0:
                            b.st_o[jh].wait(a, 16 * gi)
                        act_o.inc(a.activation(
                            O_sb[jh][:, jq * 128:(jq + 1) * 128],
                            ps_O[j % 2][:, 0:128], AF.Copy))
                    if has_next:
                        while emitted < prod_due(i - 1):
                            emit_evac(a, gi + 1, emitted)
                            emitted += 1
                if has_next:
                    while emitted < NPR:
                        emit_evac(a, gi + 1, emitted)
                        emitted += 1
        b.blk.scalar(p2_scalar)

        def p2_vector(v):
            bgk = [0]

            def bg_sub(upto):
                while bgk[0] < min(upto, NBG):
                    k = bgk[0]
                    d_bgin.wait(v, 16 * (k + 1))
                    p_bg.inc(v.tensor_scalar_add(bg_st[k % 2][:],
                                                 bg_st[k % 2][:], -128.0))
                    bgk[0] += 1

            for gi in range(NG):
                base = NIT * gi
                for i in range(NIT + 1):
                    it = base + i
                    qt = i // 2
                    if 0 <= i - 1 < NIT:
                        j = it - 1
                        act_e.wait(v, j + 1)
                        pe_t.wait(v, j - 1)
                        dve_l.inc(v.reciprocal(li_sb[j % 2][:], l_sb[j % 2][:]))
                        dve_l.wait(v, dve_l.n)
                        dve_p.inc(v.tensor_scalar(dg_sb[j % 2][:], b.ident[:],
                                                  li_sb[j % 2][:], None, ALU.mult))
                    if i < NIT:
                        pe_s.wait(v, it + 1)
                        act_e.wait(v, it - 1)
                        mi = min(qt, 2)
                        v.tensor_add(Z_sb[it % 2][:], ps_S[it % 2][:, 0:400],
                                     b.mask_sb[:, mi * 400:(mi + 1) * 400])
                        dve_z.inc(v.reduce_max(nm_sb[it % 2][:], Z_sb[it % 2][:],
                                               axis=AX.X, negate=True))
                    bg_sub(NBGG * gi + ((i + 1) * NBGG) // (NIT + 1))
        b.blk.vector(p2_vector)

        def p2_sync(sy):
            for gi in range(NG):
                for hh in range(2):
                    head = gi * 2 + hh
                    act_o.wait(sy, NIT * gi + 31 + hh)
                    b.st_o[hh].inc(sy.dma_start(b.osp[:, head, :], O_sb[hh][:]), 16)
            sy.wait_ge(b.st_o[0].h, b.st_o[0].n)
            sy.wait_ge(b.st_o[1].h, b.st_o[1].n)
        b.blk.sync(p2_sync)


    def emit_p2c(self, p2c):
        b = self
        O_sl = [b.sbt(p2c, "O_sl0", [128, NH * 512], BF16),
                b.sbt(p2c, "O_sl1", [128, NH * 512], BF16)]
        xn_sl = [b.sbt(p2c, "xn_sl0", [128, NCH * 512], BF16),
                 b.sbt(p2c, "xn_sl1", [128, NCH * 512], BF16)]
        wo_all = b.sbt(p2c, "wo_all", [128, NCH * D], BF16)
        s_sb = [b.sbt(p2c, "s_sb0", [128, 512]), b.sbt(p2c, "s_sb1", [128, 512])]
        ps_s2 = [b.pst(p2c, "ps_s20", [128, 512]), b.pst(p2c, "ps_s21", [128, 512])]

        d_wo = b.sem(p2c, "d_wo")
        d_ox = [b.sem(p2c, "d_ox0"), b.sem(p2c, "d_ox1")]
        st_s2 = [b.sem(p2c, "st_s20"), b.sem(p2c, "st_s21")]
        pe_g2 = b.sem(p2c, "pe_g2")
        dve_s2 = b.sem(p2c, "dve_s2")

        def p2c_gpsimd(g):
            for j in range(8):
                if j >= 1:
                    g.wait_ge(d_wo.h, 16 * j)
                d_wo.inc(g.dma_start(
                    wo_all[:].rearrange("p (c o) -> p c o", o=D)
                    [:, :, j * 256:(j + 1) * 256],
                    b.wo[:, :, j * 256:(j + 1) * 256]), 16)
        b.blk.gpsimd(p2c_gpsimd)

        def load_ox(sy, s):
            d_ox[s % 2].inc(sy.dma_start(
                O_sl[s % 2][:].rearrange("p (h t) -> p h t", t=512),
                b.osp[:, :, s * 512:(s + 1) * 512]), 16)
            d_ox[s % 2].inc(sy.dma_start(
                xn_sl[s % 2][:].rearrange("p (c t) -> p c t", t=512),
                b.xnsp[:, :, s * 512:(s + 1) * 512]), 16)

        def p2c_sync(sy):
            for s in range(4):
                pe_g2.wait(sy, 16 * (s - 1))
                dve_s2.wait(sy, 16 * (s - 1))
                load_ox(sy, s)
                for oc in range(16):
                    m = s * 16 + oc
                    dve_s2.wait(sy, m + 1)
                    st_s2[m % 2].inc(sy.dma_start(
                        b.ssp[:, oc, s * 512:(s + 1) * 512], s_sb[m % 2][:]), 16)
            sy.wait_ge(st_s2[0].h, st_s2[0].n)
            sy.wait_ge(st_s2[1].h, st_s2[1].n)
        b.blk.sync(p2c_sync)

        def p2c_tensor(t):
            for s in range(4):
                d_ox[s % 2].wait(t, 32 * (s // 2 + 1))
                for oc in range(16):
                    m = s * 16 + oc
                    d_wo.wait(t, 16 * (oc // 2 + 1))
                    dve_s2.wait(t, m - 1)
                    for c in range(NCH):
                        mm = t.matmul(ps_s2[m % 2][:, 0:512],
                                      wo_all[:, c * D + oc * 128:c * D + (oc + 1) * 128],
                                      O_sl[s % 2][:, c * 512:(c + 1) * 512],
                                      start=(c == 0), stop=(c == NCH - 1))
                    pe_g2.inc(mm)
        b.blk.tensor(p2c_tensor)

        def p2c_vector(v):
            for s in range(4):
                d_ox[s % 2].wait(v, 32 * (s // 2 + 1))
                for oc in range(16):
                    m = s * 16 + oc
                    pe_g2.wait(v, m + 1)
                    st_s2[m % 2].wait(v, 16 * (m // 2))
                    dve_s2.inc(v.tensor_add(
                        s_sb[m % 2][:], ps_s2[m % 2][:, 0:512],
                        xn_sl[s % 2][:, oc * 512:(oc + 1) * 512]))
        b.blk.vector(p2c_vector)


    def emit_p3(self, p3):
        b = self
        hT = b.sbt(p3, "hT", [128, NIC * 512], BF16)
        xns = b.sbt(p3, "xns", [128, NCH * 512], BF16)
        wbuf = b.sbt(p3, "wbuf", [128, 4 * NCH * 512], BF16)
        sg_sb = [b.sbt(p3, "sg_sb0", [128, 512]), b.sbt(p3, "sg_sb1", [128, 512])]
        uu_sb = [b.sbt(p3, "uu_sb0", [128, 512]), b.sbt(p3, "uu_sb1", [128, 512])]
        gf_sb = [b.sbt(p3, "gf_sb0", [128, 512]), b.sbt(p3, "gf_sb1", [128, 512])]
        x_c = [b.sbt(p3, "x_c0", [128, 512]), b.sbt(p3, "x_c1", [128, 512])]
        s_c = [b.sbt(p3, "s_c0", [128, 512]), b.sbt(p3, "s_c1", [128, 512])]
        o_c = [b.sbt(p3, "o_c0", [128, 512]), b.sbt(p3, "o_c1", [128, 512])]
        t3_sb = b.sbt(p3, "t3_sb", [128, 512])
        ps_g = [b.pst(p3, "ps_g0", [128, 512]), b.pst(p3, "ps_g1", [128, 512])]
        ps_u = [b.pst(p3, "ps_u0", [128, 512]), b.pst(p3, "ps_u1", [128, 512])]
        ps_f = [b.pst(p3, "ps_f0", [128, 512]), b.pst(p3, "ps_f1", [128, 512])]

        d_xns = b.sem(p3, "d_xns")
        d_wgu = b.sem(p3, "d_wgu")
        d_wd = b.sem(p3, "d_wd")
        d_xs = [b.sem(p3, "d_xs0"), b.sem(p3, "d_xs1")]
        st_out = [b.sem(p3, "st_out0"), b.sem(p3, "st_out1")]
        pe_gu = b.sem(p3, "pe_gu")
        act_gu = b.sem(p3, "act_gu")
        dve_h = b.sem(p3, "dve_h")
        pe_dn = b.sem(p3, "pe_dn")
        act_dn = b.sem(p3, "act_dn")
        dve_o = b.sem(p3, "dve_o")
        self.st_out = st_out

        def p3_sync(sy):
            for s in range(4):
                pe_gu.wait(sy, 128 * s)
                d_xns.inc(sy.dma_start(
                    xns[:].rearrange("p (c t) -> p c t", t=512),
                    b.xnsp[:, :, s * 512:(s + 1) * 512]), 16)
                for ib in range(16):
                    q = (ib % 2) * 2
                    pe_gu.wait(sy, 128 * s + 8 * (ib - 1))
                    if ib <= 1:
                        pe_dn.wait(sy, 16 * s)
                    b.bg.wait(sy, 16 * (4 * ib + 2))
                    d_wgu.inc(sy.dma_start(
                        wbuf[:, q * NCH * 512:(q + 1) * NCH * 512]
                        .rearrange("p (c i) -> p c i", i=512),
                        b.gbf[:, :, ib * 512:(ib + 1) * 512]), 16)
                    b.bg.wait(sy, 16 * (4 * ib + 4))
                    d_wgu.inc(sy.dma_start(
                        wbuf[:, (q + 1) * NCH * 512:(q + 2) * NCH * 512]
                        .rearrange("p (c i) -> p c i", i=512),
                        b.ubf[:, :, ib * 512:(ib + 1) * 512]), 16)
                for hc in range(16):
                    m = 16 * s + hc
                    pe_gu.wait(sy, 128 * (s + 1))
                    pe_dn.wait(sy, m - 1)
                    b.bg.wait(sy, 16 * (64 + 2 * hc + 2))
                    d_wd.inc(sy.dma_start(
                        wbuf[:, (hc % 2) * NIC * 128:((hc % 2) + 1) * NIC * 128]
                        .rearrange("p (ci h) -> p ci h", h=128),
                        b.dbf[hc]), 16)
        b.blk.sync(p3_sync)

        def p3_gpsimd(g):
            for s in range(4):
                for hc in range(16):
                    m = 16 * s + hc
                    dve_o.wait(g, m - 1)
                    d_xs[hc % 2].inc(g.dma_start(
                        x_c[hc % 2][:],
                        b.xT[:, hc, HALO + s * 512:HALO + (s + 1) * 512]), 16)
                    d_xs[hc % 2].inc(g.dma_start(
                        s_c[hc % 2][:],
                        b.ssp[:, hc, s * 512:(s + 1) * 512]), 16)
                    dve_o.wait(g, m + 1)
                    st_out[hc % 2].inc(g.dma_start(
                        b.outT[:, hc, s * 512:(s + 1) * 512], o_c[hc % 2][:]), 16)
        b.blk.gpsimd(p3_gpsimd)

        def p3_tensor(t):
            for s in range(4):
                d_xns.wait(t, 16 * (s + 1))
                for ib in range(16):
                    q = (ib % 2) * 2
                    for ic in range(4):
                        gq = 128 * s + 8 * ib + 2 * ic
                        d_wgu.wait(t, 16 * (2 * (16 * s + ib) + 1))
                        act_gu.wait(t, gq - 2)
                        for c in range(NCH):
                            mm = t.matmul(
                                ps_g[ic % 2][:, 0:512],
                                wbuf[:, q * NCH * 512 + c * 512 + ic * 128:
                                     q * NCH * 512 + c * 512 + (ic + 1) * 128],
                                xns[:, c * 512:(c + 1) * 512],
                                start=(c == 0), stop=(c == NCH - 1))
                        pe_gu.inc(mm)
                        d_wgu.wait(t, 16 * (2 * (16 * s + ib) + 2))
                        act_gu.wait(t, gq - 1)
                        for c in range(NCH):
                            mm = t.matmul(
                                ps_u[ic % 2][:, 0:512],
                                wbuf[:, (q + 1) * NCH * 512 + c * 512 + ic * 128:
                                     (q + 1) * NCH * 512 + c * 512 + (ic + 1) * 128],
                                xns[:, c * 512:(c + 1) * 512],
                                start=(c == 0), stop=(c == NCH - 1))
                        pe_gu.inc(mm)
                dve_h.wait(t, 64 * (s + 1))
                for hc in range(16):
                    m = 16 * s + hc
                    d_wd.wait(t, 16 * (m + 1))
                    act_dn.wait(t, m - 1)
                    for ci in range(NIC):
                        mm = t.matmul(ps_f[hc % 2][:, 0:512],
                                      wbuf[:, (hc % 2) * NIC * 128 + ci * 128:
                                           (hc % 2) * NIC * 128 + (ci + 1) * 128],
                                      hT[:, ci * 512:(ci + 1) * 512],
                                      start=(ci == 0), stop=(ci == NIC - 1))
                    pe_dn.inc(mm)
        b.blk.tensor(p3_tensor)

        def p3_scalar(a):
            for s in range(4):
                for ib in range(16):
                    for ic in range(4):
                        gq = 128 * s + 8 * ib + 2 * ic
                        icg = ib * 4 + ic
                        pe_gu.wait(a, gq + 1)
                        dve_h.wait(a, (gq // 2) - 1)
                        act_gu.inc(a.activation(sg_sb[icg % 2][:],
                                                ps_g[ic % 2][:, 0:512], AF.Silu,
                                                scale=b.gsc_sb[:, icg:icg + 1]))
                        pe_gu.wait(a, gq + 2)
                        act_gu.inc(a.activation(uu_sb[icg % 2][:],
                                                ps_u[ic % 2][:, 0:512], AF.Copy,
                                                scale=b.usc_sb[:, icg:icg + 1]))
                for hc in range(16):
                    m = 16 * s + hc
                    pe_dn.wait(a, m + 1)
                    dve_o.wait(a, m - 1)
                    act_dn.inc(a.activation(gf_sb[hc % 2][:],
                                            ps_f[hc % 2][:, 0:512], AF.Silu,
                                            scale=b.dsc_sb[:, hc:hc + 1]))
        b.blk.scalar(p3_scalar)

        def p3_vector(v):
            for s in range(4):
                pe_dn.wait(v, 16 * s)
                for ib in range(16):
                    for ic in range(4):
                        icg = ib * 4 + ic
                        gq = 128 * s + 8 * ib + 2 * ic
                        act_gu.wait(v, gq + 2)
                        dve_h.inc(v.tensor_mul(
                            hT[:, icg * 512:(icg + 1) * 512],
                            sg_sb[icg % 2][:], uu_sb[icg % 2][:]))
                for hc in range(16):
                    m = 16 * s + hc
                    act_dn.wait(v, m + 1)
                    d_xs[m % 2].wait(v, 32 * (m // 2 + 1))
                    st_out[m % 2].wait(v, 16 * (m // 2))
                    v.tensor_mul(t3_sb[:], s_c[hc % 2][:], gf_sb[hc % 2][:])
                    dve_o.inc(v.tensor_add(o_c[hc % 2][:], x_c[hc % 2][:],
                                           t3_sb[:]))
        b.blk.vector(p3_vector)

    def emit_final(self):
        b = self

        def fin(sy):
            if b.stage >= 4:
                sy.wait_ge(b.st_out[0].h, b.st_out[0].n)
                sy.wait_ge(b.st_out[1].h, b.st_out[1].n)
            elif b.stage >= 2:
                sy.wait_ge(b.st_o[0].h, b.st_o[0].n)
                sy.wait_ge(b.st_o[1].h, b.st_o[1].n)
        b.blk.sync(fin)



def _shared_inputs(inputs):
    f32 = np.float32
    Wq = np.asarray(inputs["L0_attn_Wq"], f32)
    Wk = np.asarray(inputs["L0_attn_Wk"], f32)
    Wv = np.asarray(inputs["L0_attn_Wv"], f32)
    Wo = np.asarray(inputs["L0_attn_Wo"], f32)
    gw = np.asarray(inputs["L0_ffn_gate_w"], np.int32)
    uw = np.asarray(inputs["L0_ffn_up_w"], np.int32)
    dw = np.asarray(inputs["L0_ffn_down_w"], np.int32)

    def pmaj(wt):
        return np.ascontiguousarray(wt.reshape(NCH, 128, wt.shape[1]).transpose(1, 0, 2))

    sh = {
        "wq": pmaj(Wq.T), "wk": pmaj(Wk.T), "wv": pmaj(Wv.T), "wo": pmaj(Wo.T),
        "gate": pmaj(gw.T), "upw": pmaj(uw.T),
        "down": np.ascontiguousarray(
            dw.T.reshape(NIC, 128, NCH, 128).transpose(2, 1, 0, 3)),
        "gT": np.ascontiguousarray(
            np.asarray(inputs["L0_attn_global_tokens"], f32)[0].T
            .reshape(NCH, 128, G).transpose(1, 0, 2)),
        "gamma": np.ascontiguousarray(
            np.asarray(inputs["L0_attn_ln_scale"], f32).reshape(NCH, 128).T),
        "beta": np.ascontiguousarray(
            np.asarray(inputs["L0_attn_ln_bias"], f32).reshape(NCH, 128).T),
        "gsc": np.ascontiguousarray(
            np.asarray(inputs["L0_ffn_gate_scale"], f32).reshape(NIC, 128).T),
        "usc": np.ascontiguousarray(
            np.asarray(inputs["L0_ffn_up_scale"], f32).reshape(NIC, 128).T),
        "dsc": np.ascontiguousarray(
            np.asarray(inputs["L0_ffn_down_scale"], f32).reshape(NCH, 128).T),
        "identin": np.eye(128, dtype=ml_dtypes.bfloat16),
    }
    return sh


def _masks_for(half):
    r = np.arange(128)[:, None]
    cc = np.arange(384)[None, :]
    band = (cc >= r + 1) & (cc <= r + 256)
    out = np.full((128, 3, 400), NEG, np.float32)
    out[:, :, 384:400] = 0.0
    for mi in range(3):
        allowed = band.copy()
        if half == 0 and mi == 0:
            allowed &= cc >= 256
        elif half == 0 and mi == 1:
            allowed &= cc >= 128
        out[:, mi, 0:384] = np.where(allowed, 0.0, NEG)
    return out


def _core_x(x, b, half):
    xb = x[b]
    seg = np.zeros((TK, D), np.float32)
    seg[HALO:] = xb[half * TOWN:(half + 1) * TOWN]
    if half == 1:
        seg[:HALO] = xb[TOWN - HALO:TOWN]
    return np.ascontiguousarray(seg.T.reshape(NCH, 128, TK).transpose(1, 0, 2))


_NC_CACHE = {}


def _get_nc(stage=4):
    if stage not in _NC_CACHE:
        _NC_CACHE[stage] = Builder(stage).build()
    return _NC_CACHE[stage]


def kernel(**inputs):
    x = np.asarray(inputs["L0_attn_x"], np.float32)
    sh = _shared_inputs(inputs)
    masks = {h: _masks_for(h) for h in (0, 1)}
    in_maps = []
    for core in range(8):
        b, half = core // 2, core % 2
        m = dict(sh)
        m["xT"] = _core_x(x, b, half)
        m["masks"] = masks[half]
        in_maps.append(m)
    nc = _get_nc(4)
    res = run_bass_kernel_spmd(nc, in_maps, core_ids=list(range(8)))
    out = np.empty((4, 4096, 2048), np.float32)
    for core in range(8):
        b, half = core // 2, core % 2
        o = np.asarray(res.results[core]["outT"])
        out[b, half * TOWN:(half + 1) * TOWN, :] = (
            o.transpose(2, 1, 0).reshape(TOWN, D))
    return out


# version 15 (final)
# speedup vs baseline: 1.2002x; 1.2002x over previous
import math
import numpy as np
import ml_dtypes
from contextlib import ExitStack

import concourse.bass as bass
import concourse.mybir as mybir
from concourse.bass_utils import run_bass_kernel_spmd

F32 = mybir.dt.float32
BF16 = mybir.dt.bfloat16
I32 = mybir.dt.int32
AF = mybir.ActivationFunctionType
ALU = mybir.AluOpType
AX = mybir.AxisListType

D = 2048
NCH = 16
TOWN = 2048
HALO = 256
TK = TOWN + HALO
G = 16
TKG = TK + G
NH = 16
HD = 128
NG = 8
NQT = 16
NIT = 32
I_FF = 8192
NIC = 64
EPS = 1e-5
QSCALE = 1.0 / math.sqrt(HD)
NEG = -1e9
P1_SLABS = [(0, 512), (512, 512), (1024, 512), (1536, 512), (2048, 256)]


class SemC:

    def __init__(self, handle):
        self.h = handle
        self.n = 0

    def inc(self, instr, k=1):
        instr.then_inc(self.h, k)
        self.n += k
        return instr

    def wait(self, eng, val):
        if val > 0:
            eng.wait_ge(self.h, val)


class Builder:
    def __init__(self, stage=4):
        self.stage = stage
        self.nc = bass.Bass("TRN2", target_bir_lowering=False,
                            detect_race_conditions=False)

    def declare_dram(self):
        nc, stage = self.nc, self.stage
        dbg = stage < 4
        EXT, INT = "ExternalOutput", "Internal"
        di = lambda n, s, d: nc.dram_tensor(n, s, d, kind="ExternalInput").ap()
        self.xT = di("xT", [128, NCH, TK], F32)
        self.gT = di("gT", [128, NCH, G], F32)
        self.wq = di("wq", [128, NCH, D], F32)
        self.wk = di("wk", [128, NCH, D], F32)
        self.wv = di("wv", [128, NCH, D], F32)
        self.wo = di("wo", [128, NCH, D], F32)
        self.gate = di("gate", [128, NCH, I_FF], I32)
        self.upw = di("upw", [128, NCH, I_FF], I32)
        self.down = di("down", [NCH, 128, NIC, 128], I32)
        self.masks = di("masks", [128, 3, 400], F32)
        self.gamma = di("gamma", [128, NCH], F32)
        self.beta = di("beta", [128, NCH], F32)
        self.gsc = di("gsc", [128, NIC], F32)
        self.usc = di("usc", [128, NIC], F32)
        self.dsc = di("dsc", [128, NCH], F32)
        self.identin = di("identin", [128, 128], BF16)

        self.outT = nc.dram_tensor("outT", [128, NCH, TOWN], F32,
                                   kind=(EXT if stage >= 4 else INT)).ap()
        self.xnsp = nc.dram_tensor("xnsp", [128, NCH, TOWN], BF16,
                                   kind=(EXT if dbg else INT)).ap()
        self.osp = nc.dram_tensor("osp", [128, NH, TOWN], BF16,
                                  kind=(EXT if (dbg and stage >= 2) else INT)).ap()
        self.ssp = nc.dram_tensor("ssp", [128, NCH, TOWN], F32,
                                  kind=(EXT if (dbg and stage >= 3) else INT)).ap()
        self.gbf = nc.dram_tensor("gbf", [128, NCH, I_FF], BF16, kind=INT).ap()
        self.ubf = nc.dram_tensor("ubf", [128, NCH, I_FF], BF16, kind=INT).ap()
        self.dbf = nc.dram_tensor("dbf", [NCH, 128, NIC, 128], BF16, kind=INT).ap()
        if dbg:
            self.xn2sp = nc.dram_tensor("xn2sp", [128, NCH, TKG], BF16, kind=EXT).ap()
            self.rowsp = nc.dram_tensor("rowsp", [2, TK], F32, kind=EXT).ap()
        if dbg and stage == 2:
            self.qdbg = nc.dram_tensor("qdbg", [128, 2 * TOWN], BF16, kind=EXT).ap()
            self.kdbg = nc.dram_tensor("kdbg", [128, 2 * TKG], BF16, kind=EXT).ap()
            self.vdbg = nc.dram_tensor("vdbg", [128, 19 * 256], BF16, kind=EXT).ap()
            self.edbg = nc.dram_tensor("edbg", [128, 3 * 400], F32, kind=EXT).ap()
            self.ptdbg = nc.dram_tensor("ptdbg", [128, 512 + 400], BF16, kind=EXT).ap()
            self.rdbg = nc.dram_tensor("rdbg", [128, 3], F32, kind=EXT).ap()

    def sbt(self, stack, name, shape, dt=F32):
        return stack.enter_context(self.nc.sbuf_tensor(name, shape, dt))

    def pst(self, stack, name, shape, dt=F32):
        return stack.enter_context(self.nc.psum_tensor(name, shape, dt))

    def sem(self, stack, name):
        return SemC(stack.enter_context(self.nc.semaphore(name)))

    def barrier(self):
        self.phase += 1
        tgt = 5 * self.phase
        bar = self.bar

        def mk(engname):
            def _f(eng):
                eng.sem_inc(bar.h, 1)
                eng.wait_ge(bar.h, tgt)
            return _f
        for name in ("gpsimd", "scalar", "tensor", "vector", "sync"):
            getattr(self.blk, name)(mk(name))
        bar.n += 5

    def build(self):
        self.declare_dram()
        with ExitStack() as ctx:
            self.ctx = ctx
            nc = self.nc
            self.mask_sb = self.sbt(ctx, "mask_sb", [128, 3 * 400])
            self.gamma_sb = self.sbt(ctx, "gamma_sb", [128, NCH])
            self.beta_sb = self.sbt(ctx, "beta_sb", [128, NCH])
            self.gsc_sb = self.sbt(ctx, "gsc_sb", [128, NIC])
            self.usc_sb = self.sbt(ctx, "usc_sb", [128, NIC])
            self.dsc_sb = self.sbt(ctx, "dsc_sb", [128, NCH])
            self.ident = self.sbt(ctx, "ident", [128, 128], BF16)
            self.ones_f = self.sbt(ctx, "ones_f", [128, 1])
            self.onesrow_f = self.sbt(ctx, "onesrow_f", [1, 128])
            self.qscale_t = self.sbt(ctx, "qscale_t", [128, 1])
            self.eps_t = self.sbt(ctx, "eps_t", [1, 1])
            self.neghalf_t = self.sbt(ctx, "neghalf_t", [1, 1])
            self.rstd_all = self.sbt(ctx, "rstd_all", [1, TK])
            self.d2_all = self.sbt(ctx, "d2_all", [1, TK])

            self.bar = self.sem(ctx, "bar")
            self.bg = self.sem(ctx, "bg")
            self.wl = self.sem(ctx, "wl")
            self.st_o = [self.sem(ctx, "st_o0"), self.sem(ctx, "st_o1")]
            self.st_s = self.sem(ctx, "st_s")
            self.phase = 0

            with nc.Block("main") as blk:
                self.blk = blk
                with ExitStack() as p12:
                    self.xn2T = self.sbt(p12, "xn2T", [128, NCH * TKG], BF16)
                    self.emit_p0()
                    self.barrier()
                    self.emit_p1()
                    self.barrier()
                    if self.stage >= 2:
                        with ExitStack() as p2s:
                            self.emit_p2(p2s)
                        self.barrier()
                if self.stage >= 3:
                    with ExitStack() as p2cs:
                        self.emit_p2c(p2cs)
                    self.barrier()
                if self.stage >= 4:
                    with ExitStack() as p3s:
                        self.emit_p3(p3s)
                self.emit_final()
        return self.nc

    def emit_p0(self):
        ctx = self.ctx
        self.d_p0 = self.sem(ctx, "d_p0")
        self.m_p0 = self.sem(ctx, "m_p0")
        d_p0, m_p0 = self.d_p0, self.m_p0
        b = self

        def p0(g):
            for dst, src in ((b.mask_sb, b.masks), (b.gamma_sb, b.gamma),
                             (b.beta_sb, b.beta), (b.gsc_sb, b.gsc),
                             (b.usc_sb, b.usc), (b.dsc_sb, b.dsc),
                             (b.ident, b.identin)):
                if len(src.shape) == 3:
                    d_p0.inc(g.dma_start(
                        dst[:].rearrange("p (a b) -> p a b", b=src.shape[2]), src), 16)
                else:
                    d_p0.inc(g.dma_start(dst[:], src), 16)
            d_p0.inc(g.dma_start(
                b.xn2T[:].rearrange("p (c t) -> p c t", t=TKG)[:, :, TK:TKG],
                b.gT), 16)
            m_p0.inc(g.memset(b.ones_f[:], 1.0))
            m_p0.inc(g.memset(b.onesrow_f[:], 1.0))
            m_p0.inc(g.memset(b.qscale_t[:], QSCALE))
            m_p0.inc(g.memset(b.eps_t[:], EPS))
            m_p0.inc(g.memset(b.neghalf_t[:], -0.5))
            g.wait_ge(d_p0.h, d_p0.n)
            g.wait_ge(m_p0.h, m_p0.n)
        self.blk.gpsimd(p0)

    def emit_p1(self):
        b = self
        with ExitStack() as p1:
            x_sb = [b.sbt(p1, "x_sb0", [128, NCH * 512]),
                    b.sbt(p1, "x_sb1", [128, NCH * 512])]
            xn_st = b.sbt(p1, "xn_st", [128, NCH * 512], BF16)
            sq_sb = b.sbt(p1, "sq_sb", [128, 2 * 512])
            bcast = b.sbt(p1, "bcast", [128, 3 * 512])
            s1_sb = b.sbt(p1, "s1_sb", [128, 512])
            s2_sb = b.sbt(p1, "s2_sb", [128, 512])
            r_mu = b.sbt(p1, "r_mu", [1, 512])
            r_msq = b.sbt(p1, "r_msq", [1, 512])
            r_tmp = b.sbt(p1, "r_tmp", [1, 512])
            r_var = b.sbt(p1, "r_var", [1, 512])
            r_rec = b.sbt(p1, "r_rec", [1, 512])
            r_v2 = b.sbt(p1, "r_v2", [1, 512])
            r_rstd2 = b.sbt(p1, "r_rstd2", [1, 512])
            ps_sum = b.pst(p1, "ps_sum", [1, 512])
            ps_sq = b.pst(p1, "ps_sq", [1, 512])
            ps_bc = [b.pst(p1, "ps_bc0", [128, 512]), b.pst(p1, "ps_bc1", [128, 512])]

            d_x1 = b.sem(p1, "d_x1")
            d_xn = b.sem(p1, "d_xn")
            pe_sum = b.sem(p1, "pe_sum")
            pe_sq = b.sem(p1, "pe_sq")
            pe_bc = b.sem(p1, "pe_bc")
            act_sq = b.sem(p1, "act_sq")
            act_row = b.sem(p1, "act_row")
            act_bc = b.sem(p1, "act_bc")
            dve_pre = b.sem(p1, "dve_pre")
            dve_stat = b.sem(p1, "dve_stat")
            dve_row = b.sem(p1, "dve_row")
            dve_chunk = b.sem(p1, "dve_chunk")

            def p1_sync(sy):
                def load(s):
                    t0, W = P1_SLABS[s]
                    d_x1.inc(sy.dma_start(
                        x_sb[s % 2][:, 0:NCH * W].rearrange("p (c t) -> p c t", t=W),
                        b.xT[:, :, t0:t0 + W]), 16)
                load(0)
                load(1)
                for s, (t0, W) in enumerate(P1_SLABS):
                    dve_chunk.wait(sy, 16 * (s + 1))
                    o0 = max(t0, HALO)
                    so = o0 - t0
                    d_xn.inc(sy.dma_start(
                        b.xnsp[:, :, o0 - HALO:t0 + W - HALO],
                        xn_st[:, 0:NCH * W].rearrange("p (c t) -> p c t", t=W)
                        [:, :, so:W]), 16)
                    if s + 2 < len(P1_SLABS):
                        act_sq.wait(sy, 16 * (s + 1))
                        pe_sum.wait(sy, s + 1)
                        load(s + 2)
                sy.wait_ge(d_xn.h, d_xn.n)
            b.blk.sync(p1_sync)

            def p1_tensor(t):
                for s, (t0, W) in enumerate(P1_SLABS):
                    d_x1.wait(t, 16 * (s + 1))
                    dve_row.wait(t, s)
                    for c in range(NCH):
                        mm = t.matmul(ps_sum[:, 0:W], b.ones_f[:],
                                      x_sb[s % 2][:, c * W:(c + 1) * W],
                                      start=(c == 0), stop=(c == NCH - 1))
                    pe_sum.inc(mm)
                    for c in range(NCH):
                        act_sq.wait(t, 16 * s + c + 1)
                        mm = t.matmul(ps_sq[:, 0:W], b.ones_f[:],
                                      sq_sb[:, (c % 2) * 512:(c % 2) * 512 + W],
                                      start=(c == 0), stop=(c == NCH - 1))
                        pe_sq.inc(mm)
                    act_row.wait(t, 2 * (s + 1))
                    dve_stat.wait(t, s + 1)
                    rows = (b.rstd_all[0:1, t0:t0 + W], b.d2_all[0:1, t0:t0 + W],
                            r_rstd2[0:1, 0:W])
                    for j, row in enumerate(rows):
                        act_bc.wait(t, 3 * s + j - 1)
                        pe_bc.inc(t.matmul(ps_bc[j % 2][:, 0:W], b.onesrow_f[:],
                                           row, start=True, stop=True))
            b.blk.tensor(p1_tensor)

            def p1_scalar(a):
                for s, (t0, W) in enumerate(P1_SLABS):
                    d_x1.wait(a, 16 * (s + 1))
                    for c in range(NCH):
                        pe_sq.wait(a, 16 * s + c - 1)
                        act_sq.inc(a.activation(
                            sq_sb[:, (c % 2) * 512:(c % 2) * 512 + W],
                            x_sb[s % 2][:, c * W:(c + 1) * W], AF.Square))
                    dve_pre.wait(a, s + 1)
                    a.activation(r_tmp[0:1, 0:W], r_var[0:1, 0:W], AF.Ln,
                                 bias=b.eps_t[:])
                    act_row.inc(a.activation(b.rstd_all[0:1, t0:t0 + W],
                                             r_tmp[0:1, 0:W], AF.Exp,
                                             scale=b.neghalf_t[:]))
                    a.activation(r_tmp[0:1, 0:W], r_v2[0:1, 0:W], AF.Ln,
                                 bias=b.eps_t[:])
                    act_row.inc(a.activation(r_rstd2[0:1, 0:W], r_tmp[0:1, 0:W],
                                             AF.Exp, scale=b.neghalf_t[:]))
                    dve_chunk.wait(a, 16 * s)
                    for j in range(3):
                        pe_bc.wait(a, 3 * s + j + 1)
                        act_bc.inc(a.activation(bcast[:, j * 512:j * 512 + W],
                                                ps_bc[j % 2][:, 0:W], AF.Copy))
            b.blk.scalar(p1_scalar)

            def p1_vector(v):
                for s, (t0, W) in enumerate(P1_SLABS):
                    pe_sum.wait(v, s + 1)
                    pe_sq.wait(v, 16 * (s + 1))
                    act_row.wait(v, 2 * s)
                    v.tensor_scalar(r_mu[0:1, 0:W], ps_sum[0:1, 0:W],
                                    1.0 / D, None, ALU.mult)
                    dve_row.inc(v.tensor_scalar(r_msq[0:1, 0:W], ps_sq[0:1, 0:W],
                                                1.0 / D, None, ALU.mult))
                    v.tensor_mul(r_tmp[0:1, 0:W], r_mu[0:1, 0:W], r_mu[0:1, 0:W])
                    v.tensor_sub(r_var[0:1, 0:W], r_msq[0:1, 0:W], r_tmp[0:1, 0:W])
                    v.tensor_scalar(r_rec[0:1, 0:W], r_var[0:1, 0:W],
                                    EPS, None, ALU.add)
                    v.reciprocal(r_rec[0:1, 0:W], r_rec[0:1, 0:W])
                    dve_pre.inc(v.tensor_mul(r_v2[0:1, 0:W], r_var[0:1, 0:W],
                                             r_rec[0:1, 0:W]))
                    act_row.wait(v, 2 * (s + 1))
                    dve_stat.inc(v.scalar_tensor_tensor(
                        b.d2_all[0:1, t0:t0 + W], r_mu[0:1, 0:W], -1.0,
                        b.rstd_all[0:1, t0:t0 + W], ALU.mult, ALU.mult))
                    act_bc.wait(v, 3 * (s + 1))
                    d_xn.wait(v, 16 * s)
                    for c in range(NCH):
                        xc = x_sb[s % 2][:, c * W:(c + 1) * W]
                        v.tensor_mul(s1_sb[:, 0:W], xc, bcast[:, 0:W])
                        v.tensor_add(s2_sb[:, 0:W], s1_sb[:, 0:W],
                                     bcast[:, 512:512 + W])
                        v.tensor_copy(xn_st[:, c * W:(c + 1) * W], s2_sb[:, 0:W])
                        v.tensor_mul(s1_sb[:, 0:W], s2_sb[:, 0:W],
                                     bcast[:, 2 * 512:2 * 512 + W])
                        dve_chunk.inc(v.tensor_scalar(
                            b.xn2T[:, c * TKG + t0:c * TKG + t0 + W],
                            s1_sb[:, 0:W], b.gamma_sb[:, c:c + 1],
                            b.beta_sb[:, c:c + 1], ALU.mult, ALU.add))
            b.blk.vector(p1_vector)

            if b.stage == 1:
                d_dbg = b.sem(p1, "d_dbg")

                def p1_dbg(sy):
                    dve_chunk.wait(sy, 16 * len(P1_SLABS))
                    d_dbg.inc(sy.dma_start(
                        b.xn2sp, b.xn2T[:].rearrange("p (c t) -> p c t", t=TKG)), 16)
                    d_dbg.inc(sy.dma_start(b.rowsp[0:1, :], b.rstd_all[:]), 16)
                    d_dbg.inc(sy.dma_start(b.rowsp[1:2, :], b.d2_all[:]), 16)
                    sy.wait_ge(d_dbg.h, d_dbg.n)
                b.blk.sync(p1_dbg)

    def emit_p2(self, p2):
        b = self
        b.wq_sl = b.sbt(p2, "wq_sl", [128, NCH * 256], BF16)
        b.wk_sl = b.sbt(p2, "wk_sl", [128, NCH * 256], BF16)
        b.wv_sl = b.sbt(p2, "wv_sl", [128, NCH * 256], BF16)
        Q_sb = [b.sbt(p2, "Q_sb0", [128, 2 * TOWN], BF16),
                b.sbt(p2, "Q_sb1", [128, 2 * TOWN], BF16)]
        K_sb = [b.sbt(p2, "K_sb0", [128, 2 * TKG], BF16),
                b.sbt(p2, "K_sb1", [128, 2 * TKG], BF16)]
        V_sb = [b.sbt(p2, "V_sb0", [128, 19 * 256], BF16),
                b.sbt(p2, "V_sb1", [128, 19 * 256], BF16)]
        O_sb = [b.sbt(p2, "O_sb0", [128, TOWN], BF16),
                b.sbt(p2, "O_sb1", [128, TOWN], BF16)]
        Z_sb = [b.sbt(p2, "Z_sb0", [128, 400]), b.sbt(p2, "Z_sb1", [128, 400])]
        E_sb = [b.sbt(p2, "E_sb0", [128, 400], BF16),
                b.sbt(p2, "E_sb1", [128, 400], BF16)]
        PT_sb = [b.sbt(p2, "PT_sb0", [128, 512], BF16),
                 b.sbt(p2, "PT_sb1", [128, 512], BF16)]
        dg_sb = [b.sbt(p2, "dg_sb0", [128, 128], BF16),
                 b.sbt(p2, "dg_sb1", [128, 128], BF16)]
        nm_sb = [b.sbt(p2, "nm_sb0", [128, 1]), b.sbt(p2, "nm_sb1", [128, 1])]
        l_sb = [b.sbt(p2, "l_sb0", [128, 1]), b.sbt(p2, "l_sb1", [128, 1])]
        li_sb = [b.sbt(p2, "li_sb0", [128, 1]), b.sbt(p2, "li_sb1", [128, 1])]
        ps_pr = [b.pst(p2, "ps_pr0", [128, 512]), b.pst(p2, "ps_pr1", [128, 512])]
        ps_S = [b.pst(p2, "ps_S0", [128, 512]), b.pst(p2, "ps_S1", [128, 512])]
        ps_PT = [b.pst(p2, "ps_PT0", [128, 512]), b.pst(p2, "ps_PT1", [128, 512])]
        ps_O = [b.pst(p2, "ps_O0", [128, 512]), b.pst(p2, "ps_O1", [128, 512])]

        pe_prod = b.sem(p2, "pe_prod")
        act_prod = b.sem(p2, "act_prod")
        pe_s = b.sem(p2, "pe_s")
        pe_t = b.sem(p2, "pe_t")
        pe_av = b.sem(p2, "pe_av")
        dve_z = b.sem(p2, "dve_z")
        dve_l = b.sem(p2, "dve_l")
        dve_p = b.sem(p2, "dve_p")
        act_e = b.sem(p2, "act_e")
        act_pt = b.sem(p2, "act_pt")
        act_o = b.sem(p2, "act_o")
        NPR = 37

        def prod_due(i):
            return min(NPR, ((i + 1) * NPR + NIT + 1) // (NIT + 2))

        NBG = 96
        NBGG = NBG // NG
        bg_st = [b.sbt(p2, "bg_st0", [128, 4096], BF16),
                 b.sbt(p2, "bg_st1", [128, 4096], BF16)]
        d_bgin = b.sem(p2, "d_bgin")
        p_bg = b.sem(p2, "p_bg")

        def bg_srcdst(k):
            if k < 64:
                ib, r = k // 4, k % 4
                j = 2 * ib + (r % 2)
                src, dst = (b.gate, b.gbf) if r < 2 else (b.upw, b.ubf)
                return (src[:, :, j * 256:(j + 1) * 256],
                        dst[:, :, j * 256:(j + 1) * 256], 256)
            kk = k - 64
            hc, h = kk // 2, kk % 2
            return (b.down[hc, :, h * 32:(h + 1) * 32, :],
                    b.dbf[hc, :, h * 32:(h + 1) * 32, :], 128)

        def bg_issue_in(g, k):
            src, _, inner = bg_srcdst(k)
            d_bgin.inc(g.dma_start(
                bg_st[k % 2][:].rearrange("p (a i) -> p a i", i=inner), src), 16)

        def bg_step(g, k):
            _, dst, inner = bg_srcdst(k)
            p_bg.wait(g, k + 1)
            b.bg.inc(g.dma_start(
                dst, bg_st[k % 2][:].rearrange("p (a i) -> p a i", i=inner)), 16)
            if k + 2 < NBG:
                g.wait_ge(b.bg.h, b.bg.n)
                bg_issue_in(g, k + 2)

        def slice_dma(g, gi, which):
            w, sl = ((b.wq, b.wq_sl), (b.wk, b.wk_sl), (b.wv, b.wv_sl))[which]
            b.wl.inc(g.dma_start(
                sl[:].rearrange("p (c o) -> p c o", o=256),
                w[:, :, gi * 256:(gi + 1) * 256]), 16)

        def p2_gpsimd(g):
            for which in range(3):
                slice_dma(g, 0, which)
            bg_issue_in(g, 0)
            bg_issue_in(g, 1)
            for which, gate in ((0, 8), (1, 18), (2, 37)):
                pe_prod.wait(g, gate)
                slice_dma(g, 1, which)
            for gi in range(NG):
                for idx, k in enumerate(range(NBGG * gi, NBGG * (gi + 1))):
                    bg_step(g, k)
                    if gi + 2 < NG:
                        if idx == 4:
                            pe_prod.wait(g, NPR * (gi + 1) + 8)
                            slice_dma(g, gi + 2, 0)
                        elif idx == 6:
                            pe_prod.wait(g, NPR * (gi + 1) + 18)
                            slice_dma(g, gi + 2, 1)
                if gi + 2 < NG:
                    pe_prod.wait(g, NPR * (gi + 2))
                    slice_dma(g, gi + 2, 2)
            g.wait_ge(b.bg.h, b.bg.n)
        b.blk.gpsimd(p2_gpsimd)

        def prod_pairs(gi):
            out = []
            e = gi % 2
            for hh in range(2):
                for sl in range(4):
                    pairs = [(b.wq_sl[:, c * 256 + hh * 128:c * 256 + (hh + 1) * 128],
                              b.xn2T[:, c * TKG + HALO + sl * 512:c * TKG + HALO + (sl + 1) * 512])
                             for c in range(NCH)]
                    out.append((512, pairs,
                                Q_sb[e][:, hh * TOWN + sl * 512:hh * TOWN + (sl + 1) * 512],
                                b.qscale_t))
            kslabs = [(0, 512), (512, 512), (1024, 512), (1536, 512), (2048, 272)]
            for hh in range(2):
                for (t0, W) in kslabs:
                    pairs = [(b.wk_sl[:, c * 256 + hh * 128:c * 256 + (hh + 1) * 128],
                              b.xn2T[:, c * TKG + t0:c * TKG + t0 + W])
                             for c in range(NCH)]
                    out.append((W, pairs,
                                K_sb[e][:, hh * TKG + t0:hh * TKG + t0 + W], None))
            for kt in range(18):
                pairs = [(b.xn2T[:, c * TKG + kt * 128:c * TKG + (kt + 1) * 128],
                          b.wv_sl[:, c * 256:(c + 1) * 256])
                         for c in range(NCH)]
                out.append((256, pairs, V_sb[e][:, kt * 256:(kt + 1) * 256], None))
            pairs = [(b.xn2T[:, c * TKG + TK:c * TKG + TKG],
                      b.wv_sl[:, c * 256:(c + 1) * 256])
                     for c in range(NCH)]
            out.append((256, pairs, V_sb[e][0:16, 18 * 256:19 * 256], None))
            return out

        def emit_prod(t, gi, j):
            gidx = NPR * gi + j
            if j == 0:
                b.wl.wait(t, 48 * gi + 16)
            elif j == 8:
                b.wl.wait(t, 48 * gi + 32)
            elif j == 18:
                b.wl.wait(t, 48 * gi + 48)
            act_prod.wait(t, gidx if j == 0 else gidx - 1)
            W, pairs, dst, scale = prod_groups[gi][j]
            for k, (lh, rh) in enumerate(pairs):
                mm = t.matmul(ps_pr[j % 2][0:dst.shape[0], 0:W], lh, rh,
                              start=(k == 0), stop=(k == len(pairs) - 1))
            pe_prod.inc(mm)

        def emit_evac(a, gi, j):
            gidx = NPR * gi + j
            pe_prod.wait(a, gidx + 1)
            W, pairs, dst, scale = prod_groups[gi][j]
            rows = dst.shape[0]
            if scale is not None:
                act_prod.inc(a.activation(dst, ps_pr[j % 2][0:rows, 0:W],
                                          AF.Copy, scale=scale[0:rows, :]))
            else:
                act_prod.inc(a.activation(dst, ps_pr[j % 2][0:rows, 0:W],
                                          AF.Copy))

        prod_groups = [prod_pairs(gi) for gi in range(NG)]

        def p2_tensor(t):
            for j in range(NPR):
                emit_prod(t, 0, j)
            for gi in range(NG):
                base = NIT * gi
                emitted = 0
                has_next = gi + 1 < NG
                for i in range(NIT + 2):
                    it = base + i
                    qt, hh = i // 2, i % 2
                    if i == 0:
                        act_prod.wait(t, NPR * gi + 18)
                    if i == 2:
                        act_prod.wait(t, NPR * (gi + 1))
                    if i < NIT:
                        dve_z.wait(t, it - 1)
                        lq = Q_sb[gi % 2][:, hh * TOWN + qt * 128:hh * TOWN + (qt + 1) * 128]
                        t.matmul(ps_S[it % 2][:, 0:384], lq,
                                 K_sb[gi % 2][:, hh * TKG + qt * 128:hh * TKG + qt * 128 + 384],
                                 start=True, stop=True)
                        pe_s.inc(t.matmul(ps_S[it % 2][:, 384:400], lq,
                                          K_sb[gi % 2][:, hh * TKG + TK:hh * TKG + TKG],
                                          start=True, stop=True))
                    if 0 <= i - 1 < NIT:
                        j = it - 1
                        act_e.wait(t, j + 1)
                        dve_p.wait(t, j + 1)
                        act_pt.wait(t, j - 1)
                        for k in range(3):
                            t.matmul(ps_PT[j % 2][:, k * 128:(k + 1) * 128],
                                     E_sb[j % 2][:, k * 128:(k + 1) * 128],
                                     dg_sb[j % 2][:], start=True, stop=True)
                        pe_t.inc(t.matmul(ps_PT[j % 2][0:16, 384:512],
                                          E_sb[j % 2][:, 384:400],
                                          dg_sb[j % 2][:], start=True, stop=True))
                    if 0 <= i - 2 < NIT:
                        j = it - 2
                        jq, jh = (i - 2) // 2, (i - 2) % 2
                        act_pt.wait(t, j + 1)
                        act_o.wait(t, j - 1)
                        for k in range(3):
                            kt = jq + k
                            t.matmul(ps_O[j % 2][:, 0:128],
                                     V_sb[gi % 2][:, kt * 256 + jh * 128:kt * 256 + (jh + 1) * 128],
                                     PT_sb[j % 2][:, k * 128:(k + 1) * 128],
                                     start=(k == 0), stop=False)
                        pe_av.inc(t.matmul(
                            ps_O[j % 2][:, 0:128],
                            V_sb[gi % 2][0:16, 18 * 256 + jh * 128:18 * 256 + (jh + 1) * 128],
                            PT_sb[j % 2][0:16, 384:512],
                            start=False, stop=True))
                    if has_next and (i % 2 == 1 or i >= NIT):
                        while emitted < prod_due(i):
                            emit_prod(t, gi + 1, emitted)
                            emitted += 1
        b.blk.tensor(p2_tensor)

        def p2_scalar(a):
            for j in range(NPR):
                emit_evac(a, 0, j)
            for gi in range(NG):
                base = NIT * gi
                emitted = 0
                has_next = gi + 1 < NG
                for i in range(NIT + 2):
                    it = base + i
                    if i < NIT:
                        dve_z.wait(a, it + 1)
                        pe_t.wait(a, it - 1)
                        dve_l.wait(a, it - 1)
                        act_e.inc(a.activation(E_sb[it % 2][:], Z_sb[it % 2][:],
                                               AF.Exp, bias=nm_sb[it % 2][:],
                                               accum_out=l_sb[it % 2][:]))
                    if 0 <= i - 1 < NIT:
                        j = it - 1
                        pe_t.wait(a, j + 1)
                        pe_av.wait(a, j - 1)
                        a.activation(PT_sb[j % 2][:, 0:384], ps_PT[j % 2][:, 0:384],
                                     AF.Copy)
                        act_pt.inc(a.activation(PT_sb[j % 2][0:16, 384:512],
                                                ps_PT[j % 2][0:16, 384:512], AF.Copy))
                    if 0 <= i - 2 < NIT:
                        j = it - 2
                        jq, jh = (i - 2) // 2, (i - 2) % 2
                        pe_av.wait(a, j + 1)
                        if jq == 0:
                            b.st_o[jh].wait(a, 16 * gi)
                        act_o.inc(a.activation(
                            O_sb[jh][:, jq * 128:(jq + 1) * 128],
                            ps_O[j % 2][:, 0:128], AF.Copy))
                    if has_next:
                        while emitted < prod_due(i - 1):
                            emit_evac(a, gi + 1, emitted)
                            emitted += 1
                if has_next:
                    while emitted < NPR:
                        emit_evac(a, gi + 1, emitted)
                        emitted += 1
        b.blk.scalar(p2_scalar)

        def p2_vector(v):
            bgk = [0]

            def bg_sub(upto):
                while bgk[0] < min(upto, NBG):
                    k = bgk[0]
                    d_bgin.wait(v, 16 * (k + 1))
                    p_bg.inc(v.tensor_scalar_add(bg_st[k % 2][:],
                                                 bg_st[k % 2][:], -128.0))
                    bgk[0] += 1

            for gi in range(NG):
                base = NIT * gi
                for i in range(NIT + 1):
                    it = base + i
                    qt = i // 2
                    if 0 <= i - 1 < NIT:
                        j = it - 1
                        act_e.wait(v, j + 1)
                        pe_t.wait(v, j - 1)
                        dve_l.inc(v.reciprocal(li_sb[j % 2][:], l_sb[j % 2][:]))
                        dve_l.wait(v, dve_l.n)
                        dve_p.inc(v.tensor_scalar(dg_sb[j % 2][:], b.ident[:],
                                                  li_sb[j % 2][:], None, ALU.mult))
                    if i < NIT:
                        pe_s.wait(v, it + 1)
                        act_e.wait(v, it - 1)
                        mi = min(qt, 2)
                        v.tensor_add(Z_sb[it % 2][:], ps_S[it % 2][:, 0:400],
                                     b.mask_sb[:, mi * 400:(mi + 1) * 400])
                        dve_z.inc(v.reduce_max(nm_sb[it % 2][:], Z_sb[it % 2][:],
                                               axis=AX.X, negate=True))
                    bg_sub(NBGG * gi + ((i + 1) * NBGG) // (NIT + 1))
        b.blk.vector(p2_vector)

        def p2_sync(sy):
            for gi in range(NG):
                for hh in range(2):
                    head = gi * 2 + hh
                    act_o.wait(sy, NIT * gi + 31 + hh)
                    b.st_o[hh].inc(sy.dma_start(b.osp[:, head, :], O_sb[hh][:]), 16)
            sy.wait_ge(b.st_o[0].h, b.st_o[0].n)
            sy.wait_ge(b.st_o[1].h, b.st_o[1].n)
        b.blk.sync(p2_sync)


    def emit_p2c(self, p2c):
        b = self
        O_sl = [b.sbt(p2c, "O_sl0", [128, NH * 512], BF16),
                b.sbt(p2c, "O_sl1", [128, NH * 512], BF16)]
        xn_sl = [b.sbt(p2c, "xn_sl0", [128, NCH * 512], BF16),
                 b.sbt(p2c, "xn_sl1", [128, NCH * 512], BF16)]
        wo_all = b.sbt(p2c, "wo_all", [128, NCH * D], BF16)
        s_sb = [b.sbt(p2c, "s_sb0", [128, 512]), b.sbt(p2c, "s_sb1", [128, 512])]
        ps_s2 = [b.pst(p2c, "ps_s20", [128, 512]), b.pst(p2c, "ps_s21", [128, 512])]

        d_wo = b.sem(p2c, "d_wo")
        d_ox = [b.sem(p2c, "d_ox0"), b.sem(p2c, "d_ox1")]
        st_s2 = [b.sem(p2c, "st_s20"), b.sem(p2c, "st_s21")]
        pe_g2 = b.sem(p2c, "pe_g2")
        dve_s2 = b.sem(p2c, "dve_s2")

        def p2c_gpsimd(g):
            for j in range(8):
                if j >= 1:
                    g.wait_ge(d_wo.h, 16 * j)
                d_wo.inc(g.dma_start(
                    wo_all[:].rearrange("p (c o) -> p c o", o=D)
                    [:, :, j * 256:(j + 1) * 256],
                    b.wo[:, :, j * 256:(j + 1) * 256]), 16)
        b.blk.gpsimd(p2c_gpsimd)

        def load_ox(sy, s):
            d_ox[s % 2].inc(sy.dma_start(
                O_sl[s % 2][:].rearrange("p (h t) -> p h t", t=512),
                b.osp[:, :, s * 512:(s + 1) * 512]), 16)
            d_ox[s % 2].inc(sy.dma_start(
                xn_sl[s % 2][:].rearrange("p (c t) -> p c t", t=512),
                b.xnsp[:, :, s * 512:(s + 1) * 512]), 16)

        def p2c_sync(sy):
            load_ox(sy, 0)
            load_ox(sy, 1)
            for s in range(4):
                for oc in range(16):
                    m = s * 16 + oc
                    dve_s2.wait(sy, m + 1)
                    st_s2[m % 2].inc(sy.dma_start(
                        b.ssp[:, oc, s * 512:(s + 1) * 512], s_sb[m % 2][:]), 16)
                if s + 2 < 4:
                    load_ox(sy, s + 2)
            sy.wait_ge(st_s2[0].h, st_s2[0].n)
            sy.wait_ge(st_s2[1].h, st_s2[1].n)
        b.blk.sync(p2c_sync)

        def p2c_tensor(t):
            for s in range(4):
                d_ox[s % 2].wait(t, 32 * (s // 2 + 1))
                for oc in range(16):
                    m = s * 16 + oc
                    d_wo.wait(t, 16 * (oc // 2 + 1))
                    dve_s2.wait(t, m - 1)
                    for c in range(NCH):
                        mm = t.matmul(ps_s2[m % 2][:, 0:512],
                                      wo_all[:, c * D + oc * 128:c * D + (oc + 1) * 128],
                                      O_sl[s % 2][:, c * 512:(c + 1) * 512],
                                      start=(c == 0), stop=(c == NCH - 1))
                    pe_g2.inc(mm)
        b.blk.tensor(p2c_tensor)

        def p2c_vector(v):
            for s in range(4):
                d_ox[s % 2].wait(v, 32 * (s // 2 + 1))
                for oc in range(16):
                    m = s * 16 + oc
                    pe_g2.wait(v, m + 1)
                    st_s2[m % 2].wait(v, 16 * (m // 2))
                    dve_s2.inc(v.tensor_add(
                        s_sb[m % 2][:], ps_s2[m % 2][:, 0:512],
                        xn_sl[s % 2][:, oc * 512:(oc + 1) * 512]))
        b.blk.vector(p2c_vector)


    def emit_p3(self, p3):
        b = self
        hT = b.sbt(p3, "hT", [128, NIC * 512], BF16)
        xns = b.sbt(p3, "xns", [128, NCH * 512], BF16)
        wbuf = b.sbt(p3, "wbuf", [128, 4 * NCH * 512], BF16)
        sg_sb = [b.sbt(p3, "sg_sb0", [128, 512]), b.sbt(p3, "sg_sb1", [128, 512])]
        uu_sb = [b.sbt(p3, "uu_sb0", [128, 512]), b.sbt(p3, "uu_sb1", [128, 512])]
        gf_sb = [b.sbt(p3, "gf_sb0", [128, 512]), b.sbt(p3, "gf_sb1", [128, 512])]
        x_c = [b.sbt(p3, "x_c0", [128, 512]), b.sbt(p3, "x_c1", [128, 512])]
        s_c = [b.sbt(p3, "s_c0", [128, 512]), b.sbt(p3, "s_c1", [128, 512])]
        o_c = [b.sbt(p3, "o_c0", [128, 512]), b.sbt(p3, "o_c1", [128, 512])]
        t3_sb = b.sbt(p3, "t3_sb", [128, 512])
        ps_g = [b.pst(p3, "ps_g0", [128, 512]), b.pst(p3, "ps_g1", [128, 512])]
        ps_u = [b.pst(p3, "ps_u0", [128, 512]), b.pst(p3, "ps_u1", [128, 512])]
        ps_f = [b.pst(p3, "ps_f0", [128, 512]), b.pst(p3, "ps_f1", [128, 512])]

        d_xns = b.sem(p3, "d_xns")
        d_wgu = b.sem(p3, "d_wgu")
        d_wd = b.sem(p3, "d_wd")
        d_xs = [b.sem(p3, "d_xs0"), b.sem(p3, "d_xs1")]
        st_out = [b.sem(p3, "st_out0"), b.sem(p3, "st_out1")]
        pe_gu = b.sem(p3, "pe_gu")
        act_gu = b.sem(p3, "act_gu")
        dve_h = b.sem(p3, "dve_h")
        pe_dn = b.sem(p3, "pe_dn")
        act_dn = b.sem(p3, "act_dn")
        dve_o = b.sem(p3, "dve_o")
        self.st_out = st_out

        def p3_sync(sy):
            for s in range(4):
                pe_gu.wait(sy, 128 * s)
                d_xns.inc(sy.dma_start(
                    xns[:].rearrange("p (c t) -> p c t", t=512),
                    b.xnsp[:, :, s * 512:(s + 1) * 512]), 16)
                for ib in range(16):
                    q = (ib % 2) * 2
                    pe_gu.wait(sy, 128 * s + 8 * (ib - 1))
                    if ib <= 1:
                        pe_dn.wait(sy, 16 * s)
                    b.bg.wait(sy, 16 * (4 * ib + 2))
                    d_wgu.inc(sy.dma_start(
                        wbuf[:, q * NCH * 512:(q + 1) * NCH * 512]
                        .rearrange("p (c i) -> p c i", i=512),
                        b.gbf[:, :, ib * 512:(ib + 1) * 512]), 16)
                    b.bg.wait(sy, 16 * (4 * ib + 4))
                    d_wgu.inc(sy.dma_start(
                        wbuf[:, (q + 1) * NCH * 512:(q + 2) * NCH * 512]
                        .rearrange("p (c i) -> p c i", i=512),
                        b.ubf[:, :, ib * 512:(ib + 1) * 512]), 16)
                for hc in range(16):
                    m = 16 * s + hc
                    pe_gu.wait(sy, 128 * (s + 1))
                    pe_dn.wait(sy, m - 1)
                    b.bg.wait(sy, 16 * (64 + 2 * hc + 2))
                    d_wd.inc(sy.dma_start(
                        wbuf[:, (hc % 2) * NIC * 128:((hc % 2) + 1) * NIC * 128]
                        .rearrange("p (ci h) -> p ci h", h=128),
                        b.dbf[hc]), 16)
        b.blk.sync(p3_sync)

        def p3_gpsimd(g):
            for s in range(4):
                for hc in range(16):
                    m = 16 * s + hc
                    dve_o.wait(g, m - 1)
                    d_xs[hc % 2].inc(g.dma_start(
                        x_c[hc % 2][:],
                        b.xT[:, hc, HALO + s * 512:HALO + (s + 1) * 512]), 16)
                    d_xs[hc % 2].inc(g.dma_start(
                        s_c[hc % 2][:],
                        b.ssp[:, hc, s * 512:(s + 1) * 512]), 16)
                    dve_o.wait(g, m + 1)
                    st_out[hc % 2].inc(g.dma_start(
                        b.outT[:, hc, s * 512:(s + 1) * 512], o_c[hc % 2][:]), 16)
        b.blk.gpsimd(p3_gpsimd)

        def p3_tensor(t):
            for s in range(4):
                d_xns.wait(t, 16 * (s + 1))
                for ib in range(16):
                    q = (ib % 2) * 2
                    for ic in range(4):
                        gq = 128 * s + 8 * ib + 2 * ic
                        d_wgu.wait(t, 16 * (2 * (16 * s + ib) + 1))
                        act_gu.wait(t, gq - 2)
                        for c in range(NCH):
                            mm = t.matmul(
                                ps_g[ic % 2][:, 0:512],
                                wbuf[:, q * NCH * 512 + c * 512 + ic * 128:
                                     q * NCH * 512 + c * 512 + (ic + 1) * 128],
                                xns[:, c * 512:(c + 1) * 512],
                                start=(c == 0), stop=(c == NCH - 1))
                        pe_gu.inc(mm)
                        d_wgu.wait(t, 16 * (2 * (16 * s + ib) + 2))
                        act_gu.wait(t, gq - 1)
                        for c in range(NCH):
                            mm = t.matmul(
                                ps_u[ic % 2][:, 0:512],
                                wbuf[:, (q + 1) * NCH * 512 + c * 512 + ic * 128:
                                     (q + 1) * NCH * 512 + c * 512 + (ic + 1) * 128],
                                xns[:, c * 512:(c + 1) * 512],
                                start=(c == 0), stop=(c == NCH - 1))
                        pe_gu.inc(mm)
                dve_h.wait(t, 64 * (s + 1))
                for hc in range(16):
                    m = 16 * s + hc
                    d_wd.wait(t, 16 * (m + 1))
                    act_dn.wait(t, m - 1)
                    for ci in range(NIC):
                        mm = t.matmul(ps_f[hc % 2][:, 0:512],
                                      wbuf[:, (hc % 2) * NIC * 128 + ci * 128:
                                           (hc % 2) * NIC * 128 + (ci + 1) * 128],
                                      hT[:, ci * 512:(ci + 1) * 512],
                                      start=(ci == 0), stop=(ci == NIC - 1))
                    pe_dn.inc(mm)
        b.blk.tensor(p3_tensor)

        def p3_scalar(a):
            for s in range(4):
                for ib in range(16):
                    for ic in range(4):
                        gq = 128 * s + 8 * ib + 2 * ic
                        icg = ib * 4 + ic
                        pe_gu.wait(a, gq + 1)
                        dve_h.wait(a, (gq // 2) - 1)
                        act_gu.inc(a.activation(sg_sb[icg % 2][:],
                                                ps_g[ic % 2][:, 0:512], AF.Silu,
                                                scale=b.gsc_sb[:, icg:icg + 1]))
                        pe_gu.wait(a, gq + 2)
                        act_gu.inc(a.activation(uu_sb[icg % 2][:],
                                                ps_u[ic % 2][:, 0:512], AF.Copy,
                                                scale=b.usc_sb[:, icg:icg + 1]))
                for hc in range(16):
                    m = 16 * s + hc
                    pe_dn.wait(a, m + 1)
                    dve_o.wait(a, m - 1)
                    act_dn.inc(a.activation(gf_sb[hc % 2][:],
                                            ps_f[hc % 2][:, 0:512], AF.Silu,
                                            scale=b.dsc_sb[:, hc:hc + 1]))
        b.blk.scalar(p3_scalar)

        def p3_vector(v):
            for s in range(4):
                pe_dn.wait(v, 16 * s)
                for ib in range(16):
                    for ic in range(4):
                        icg = ib * 4 + ic
                        gq = 128 * s + 8 * ib + 2 * ic
                        act_gu.wait(v, gq + 2)
                        dve_h.inc(v.tensor_mul(
                            hT[:, icg * 512:(icg + 1) * 512],
                            sg_sb[icg % 2][:], uu_sb[icg % 2][:]))
                for hc in range(16):
                    m = 16 * s + hc
                    act_dn.wait(v, m + 1)
                    d_xs[m % 2].wait(v, 32 * (m // 2 + 1))
                    st_out[m % 2].wait(v, 16 * (m // 2))
                    v.tensor_mul(t3_sb[:], s_c[hc % 2][:], gf_sb[hc % 2][:])
                    dve_o.inc(v.tensor_add(o_c[hc % 2][:], x_c[hc % 2][:],
                                           t3_sb[:]))
        b.blk.vector(p3_vector)

    def emit_final(self):
        b = self

        def fin(sy):
            if b.stage >= 4:
                sy.wait_ge(b.st_out[0].h, b.st_out[0].n)
                sy.wait_ge(b.st_out[1].h, b.st_out[1].n)
            elif b.stage >= 2:
                sy.wait_ge(b.st_o[0].h, b.st_o[0].n)
                sy.wait_ge(b.st_o[1].h, b.st_o[1].n)
        b.blk.sync(fin)



def _shared_inputs(inputs):
    f32 = np.float32
    Wq = np.asarray(inputs["L0_attn_Wq"], f32)
    Wk = np.asarray(inputs["L0_attn_Wk"], f32)
    Wv = np.asarray(inputs["L0_attn_Wv"], f32)
    Wo = np.asarray(inputs["L0_attn_Wo"], f32)
    gw = np.asarray(inputs["L0_ffn_gate_w"], np.int32)
    uw = np.asarray(inputs["L0_ffn_up_w"], np.int32)
    dw = np.asarray(inputs["L0_ffn_down_w"], np.int32)

    def pmaj(wt):
        return np.ascontiguousarray(wt.reshape(NCH, 128, wt.shape[1]).transpose(1, 0, 2))

    sh = {
        "wq": pmaj(Wq.T), "wk": pmaj(Wk.T), "wv": pmaj(Wv.T), "wo": pmaj(Wo.T),
        "gate": pmaj(gw.T), "upw": pmaj(uw.T),
        "down": np.ascontiguousarray(
            dw.T.reshape(NIC, 128, NCH, 128).transpose(2, 1, 0, 3)),
        "gT": np.ascontiguousarray(
            np.asarray(inputs["L0_attn_global_tokens"], f32)[0].T
            .reshape(NCH, 128, G).transpose(1, 0, 2)),
        "gamma": np.ascontiguousarray(
            np.asarray(inputs["L0_attn_ln_scale"], f32).reshape(NCH, 128).T),
        "beta": np.ascontiguousarray(
            np.asarray(inputs["L0_attn_ln_bias"], f32).reshape(NCH, 128).T),
        "gsc": np.ascontiguousarray(
            np.asarray(inputs["L0_ffn_gate_scale"], f32).reshape(NIC, 128).T),
        "usc": np.ascontiguousarray(
            np.asarray(inputs["L0_ffn_up_scale"], f32).reshape(NIC, 128).T),
        "dsc": np.ascontiguousarray(
            np.asarray(inputs["L0_ffn_down_scale"], f32).reshape(NCH, 128).T),
        "identin": np.eye(128, dtype=ml_dtypes.bfloat16),
    }
    return sh


def _masks_for(half):
    r = np.arange(128)[:, None]
    cc = np.arange(384)[None, :]
    band = (cc >= r + 1) & (cc <= r + 256)
    out = np.full((128, 3, 400), NEG, np.float32)
    out[:, :, 384:400] = 0.0
    for mi in range(3):
        allowed = band.copy()
        if half == 0 and mi == 0:
            allowed &= cc >= 256
        elif half == 0 and mi == 1:
            allowed &= cc >= 128
        out[:, mi, 0:384] = np.where(allowed, 0.0, NEG)
    return out


def _core_x(x, b, half):
    xb = x[b]
    seg = np.zeros((TK, D), np.float32)
    seg[HALO:] = xb[half * TOWN:(half + 1) * TOWN]
    if half == 1:
        seg[:HALO] = xb[TOWN - HALO:TOWN]
    return np.ascontiguousarray(seg.T.reshape(NCH, 128, TK).transpose(1, 0, 2))


_NC_CACHE = {}


def _get_nc(stage=4):
    if stage not in _NC_CACHE:
        _NC_CACHE[stage] = Builder(stage).build()
    return _NC_CACHE[stage]


def kernel(**inputs):
    x = np.asarray(inputs["L0_attn_x"], np.float32)
    sh = _shared_inputs(inputs)
    masks = {h: _masks_for(h) for h in (0, 1)}
    in_maps = []
    for core in range(8):
        b, half = core // 2, core % 2
        m = dict(sh)
        m["xT"] = _core_x(x, b, half)
        m["masks"] = masks[half]
        in_maps.append(m)
    nc = _get_nc(4)
    res = run_bass_kernel_spmd(nc, in_maps, core_ids=list(range(8)))
    out = np.empty((4, 4096, 2048), np.float32)
    for core in range(8):
        b, half = core // 2, core % 2
        o = np.asarray(res.results[core]["outT"])
        out[b, half * TOWN:(half + 1) * TOWN, :] = (
            o.transpose(2, 1, 0).reshape(TOWN, D))
    return out


# version 16
# speedup vs baseline: 1.2064x; 1.0052x over previous
import math
import numpy as np
import ml_dtypes
from contextlib import ExitStack

import concourse.bass as bass
import concourse.mybir as mybir
from concourse.bass_utils import run_bass_kernel_spmd

F32 = mybir.dt.float32
BF16 = mybir.dt.bfloat16
I32 = mybir.dt.int32
AF = mybir.ActivationFunctionType
ALU = mybir.AluOpType
AX = mybir.AxisListType

D = 2048
NCH = 16
TOWN = 2048
HALO = 256
TK = TOWN + HALO
G = 16
TKG = TK + G
NH = 16
HD = 128
NG = 8
NQT = 16
NIT = 32
I_FF = 8192
NIC = 64
EPS = 1e-5
QSCALE = 1.0 / math.sqrt(HD)
NEG = -1e9
P1_SLABS = [(0, 512), (512, 512), (1024, 512), (1536, 512), (2048, 256)]


class SemC:

    def __init__(self, handle):
        self.h = handle
        self.n = 0

    def inc(self, instr, k=1):
        instr.then_inc(self.h, k)
        self.n += k
        return instr

    def wait(self, eng, val):
        if val > 0:
            eng.wait_ge(self.h, val)


class Builder:
    def __init__(self, stage=4):
        self.stage = stage
        self.nc = bass.Bass("TRN2", target_bir_lowering=False,
                            detect_race_conditions=False)

    def declare_dram(self):
        nc, stage = self.nc, self.stage
        dbg = stage < 4
        EXT, INT = "ExternalOutput", "Internal"
        di = lambda n, s, d: nc.dram_tensor(n, s, d, kind="ExternalInput").ap()
        self.xT = di("xT", [128, NCH, TK], F32)
        self.gT = di("gT", [128, NCH, G], F32)
        self.wq = di("wq", [128, NCH, D], F32)
        self.wk = di("wk", [128, NCH, D], F32)
        self.wv = di("wv", [128, NCH, D], F32)
        self.wo = di("wo", [128, NCH, D], F32)
        self.gate = di("gate", [128, NCH, I_FF], I32)
        self.upw = di("upw", [128, NCH, I_FF], I32)
        self.down = di("down", [NCH, 128, NIC, 128], I32)
        self.masks = di("masks", [128, 3, 400], F32)
        self.gamma = di("gamma", [128, NCH], F32)
        self.beta = di("beta", [128, NCH], F32)
        self.gsc = di("gsc", [128, NIC], F32)
        self.usc = di("usc", [128, NIC], F32)
        self.dsc = di("dsc", [128, NCH], F32)
        self.identin = di("identin", [128, 128], BF16)

        self.outT = nc.dram_tensor("outT", [128, NCH, TOWN], F32,
                                   kind=(EXT if stage >= 4 else INT)).ap()
        self.xnsp = nc.dram_tensor("xnsp", [128, NCH, TOWN], BF16,
                                   kind=(EXT if dbg else INT)).ap()
        self.osp = nc.dram_tensor("osp", [128, NH, TOWN], BF16,
                                  kind=(EXT if (dbg and stage >= 2) else INT)).ap()
        self.ssp = nc.dram_tensor("ssp", [128, NCH, TOWN], F32,
                                  kind=(EXT if (dbg and stage >= 3) else INT)).ap()
        self.gbf = nc.dram_tensor("gbf", [128, NCH, I_FF], BF16, kind=INT).ap()
        self.ubf = nc.dram_tensor("ubf", [128, NCH, I_FF], BF16, kind=INT).ap()
        self.dbf = nc.dram_tensor("dbf", [NCH, 128, NIC, 128], BF16, kind=INT).ap()
        if dbg:
            self.xn2sp = nc.dram_tensor("xn2sp", [128, NCH, TKG], BF16, kind=EXT).ap()
            self.rowsp = nc.dram_tensor("rowsp", [2, TK], F32, kind=EXT).ap()
        if dbg and stage == 2:
            self.qdbg = nc.dram_tensor("qdbg", [128, 2 * TOWN], BF16, kind=EXT).ap()
            self.kdbg = nc.dram_tensor("kdbg", [128, 2 * TKG], BF16, kind=EXT).ap()
            self.vdbg = nc.dram_tensor("vdbg", [128, 19 * 256], BF16, kind=EXT).ap()
            self.edbg = nc.dram_tensor("edbg", [128, 3 * 400], F32, kind=EXT).ap()
            self.ptdbg = nc.dram_tensor("ptdbg", [128, 512 + 400], BF16, kind=EXT).ap()
            self.rdbg = nc.dram_tensor("rdbg", [128, 3], F32, kind=EXT).ap()

    def sbt(self, stack, name, shape, dt=F32):
        return stack.enter_context(self.nc.sbuf_tensor(name, shape, dt))

    def pst(self, stack, name, shape, dt=F32):
        return stack.enter_context(self.nc.psum_tensor(name, shape, dt))

    def sem(self, stack, name):
        return SemC(stack.enter_context(self.nc.semaphore(name)))

    def barrier(self):
        self.phase += 1
        tgt = 5 * self.phase
        bar = self.bar

        def mk(engname):
            def _f(eng):
                eng.sem_inc(bar.h, 1)
                eng.wait_ge(bar.h, tgt)
            return _f
        for name in ("gpsimd", "scalar", "tensor", "vector", "sync"):
            getattr(self.blk, name)(mk(name))
        bar.n += 5

    def build(self):
        self.declare_dram()
        with ExitStack() as ctx:
            self.ctx = ctx
            nc = self.nc
            self.mask_sb = self.sbt(ctx, "mask_sb", [128, 3 * 400])
            self.gamma_sb = self.sbt(ctx, "gamma_sb", [128, NCH])
            self.beta_sb = self.sbt(ctx, "beta_sb", [128, NCH])
            self.gsc_sb = self.sbt(ctx, "gsc_sb", [128, NIC])
            self.usc_sb = self.sbt(ctx, "usc_sb", [128, NIC])
            self.dsc_sb = self.sbt(ctx, "dsc_sb", [128, NCH])
            self.ident = self.sbt(ctx, "ident", [128, 128], BF16)
            self.ones_f = self.sbt(ctx, "ones_f", [128, 1])
            self.onesrow_f = self.sbt(ctx, "onesrow_f", [1, 128])
            self.qscale_t = self.sbt(ctx, "qscale_t", [128, 1])
            self.eps_t = self.sbt(ctx, "eps_t", [1, 1])
            self.neghalf_t = self.sbt(ctx, "neghalf_t", [1, 1])
            self.rstd_all = self.sbt(ctx, "rstd_all", [1, TK])
            self.d2_all = self.sbt(ctx, "d2_all", [1, TK])

            self.bar = self.sem(ctx, "bar")
            self.bg = self.sem(ctx, "bg")
            self.wl = self.sem(ctx, "wl")
            self.st_o = [self.sem(ctx, "st_o0"), self.sem(ctx, "st_o1")]
            self.st_s = self.sem(ctx, "st_s")
            self.phase = 0

            with nc.Block("main") as blk:
                self.blk = blk
                with ExitStack() as p12:
                    self.xn2T = self.sbt(p12, "xn2T", [128, NCH * TKG], BF16)
                    self.emit_p0()
                    self.barrier()
                    self.emit_p1()
                    self.barrier()
                    if self.stage >= 2:
                        with ExitStack() as p2s:
                            self.emit_p2(p2s)
                        self.barrier()
                if self.stage >= 3:
                    with ExitStack() as p2cs:
                        self.emit_p2c(p2cs)
                    self.barrier()
                if self.stage >= 4:
                    with ExitStack() as p3s:
                        self.emit_p3(p3s)
                self.emit_final()
        return self.nc

    def emit_p0(self):
        ctx = self.ctx
        self.d_p0 = self.sem(ctx, "d_p0")
        self.m_p0 = self.sem(ctx, "m_p0")
        d_p0, m_p0 = self.d_p0, self.m_p0
        b = self

        def p0(g):
            for dst, src in ((b.mask_sb, b.masks), (b.gamma_sb, b.gamma),
                             (b.beta_sb, b.beta), (b.gsc_sb, b.gsc),
                             (b.usc_sb, b.usc), (b.dsc_sb, b.dsc),
                             (b.ident, b.identin)):
                if len(src.shape) == 3:
                    d_p0.inc(g.dma_start(
                        dst[:].rearrange("p (a b) -> p a b", b=src.shape[2]), src), 16)
                else:
                    d_p0.inc(g.dma_start(dst[:], src), 16)
            d_p0.inc(g.dma_start(
                b.xn2T[:].rearrange("p (c t) -> p c t", t=TKG)[:, :, TK:TKG],
                b.gT), 16)
            m_p0.inc(g.memset(b.ones_f[:], 1.0))
            m_p0.inc(g.memset(b.onesrow_f[:], 1.0))
            m_p0.inc(g.memset(b.qscale_t[:], QSCALE))
            m_p0.inc(g.memset(b.eps_t[:], EPS))
            m_p0.inc(g.memset(b.neghalf_t[:], -0.5))
            g.wait_ge(d_p0.h, d_p0.n)
            g.wait_ge(m_p0.h, m_p0.n)
        self.blk.gpsimd(p0)

    def emit_p1(self):
        b = self
        with ExitStack() as p1:
            x_sb = [b.sbt(p1, "x_sb0", [128, NCH * 512]),
                    b.sbt(p1, "x_sb1", [128, NCH * 512])]
            xn_st = b.sbt(p1, "xn_st", [128, NCH * 512], BF16)
            sq_sb = b.sbt(p1, "sq_sb", [128, 2 * 512])
            bcast = b.sbt(p1, "bcast", [128, 3 * 512])
            s1_sb = b.sbt(p1, "s1_sb", [128, 512])
            s2_sb = b.sbt(p1, "s2_sb", [128, 512])
            r_mu = b.sbt(p1, "r_mu", [1, 512])
            r_msq = b.sbt(p1, "r_msq", [1, 512])
            r_tmp = b.sbt(p1, "r_tmp", [1, 512])
            r_var = b.sbt(p1, "r_var", [1, 512])
            r_rec = b.sbt(p1, "r_rec", [1, 512])
            r_v2 = b.sbt(p1, "r_v2", [1, 512])
            r_rstd2 = b.sbt(p1, "r_rstd2", [1, 512])
            ps_sum = b.pst(p1, "ps_sum", [1, 512])
            ps_sq = b.pst(p1, "ps_sq", [1, 512])
            ps_bc = [b.pst(p1, "ps_bc0", [128, 512]), b.pst(p1, "ps_bc1", [128, 512])]

            d_x1 = b.sem(p1, "d_x1")
            d_xn = b.sem(p1, "d_xn")
            pe_sum = b.sem(p1, "pe_sum")
            pe_sq = b.sem(p1, "pe_sq")
            pe_bc = b.sem(p1, "pe_bc")
            act_sq = b.sem(p1, "act_sq")
            act_row = b.sem(p1, "act_row")
            act_bc = b.sem(p1, "act_bc")
            dve_pre = b.sem(p1, "dve_pre")
            dve_stat = b.sem(p1, "dve_stat")
            dve_row = b.sem(p1, "dve_row")
            dve_chunk = b.sem(p1, "dve_chunk")

            def p1_sync(sy):
                def load(s):
                    t0, W = P1_SLABS[s]
                    d_x1.inc(sy.dma_start(
                        x_sb[s % 2][:, 0:NCH * W].rearrange("p (c t) -> p c t", t=W),
                        b.xT[:, :, t0:t0 + W]), 16)
                load(0)
                load(1)
                for s, (t0, W) in enumerate(P1_SLABS):
                    dve_chunk.wait(sy, 16 * (s + 1))
                    o0 = max(t0, HALO)
                    so = o0 - t0
                    d_xn.inc(sy.dma_start(
                        b.xnsp[:, :, o0 - HALO:t0 + W - HALO],
                        xn_st[:, 0:NCH * W].rearrange("p (c t) -> p c t", t=W)
                        [:, :, so:W]), 16)
                    if s + 2 < len(P1_SLABS):
                        act_sq.wait(sy, 16 * (s + 1))
                        pe_sum.wait(sy, s + 1)
                        load(s + 2)
                sy.wait_ge(d_xn.h, d_xn.n)
            b.blk.sync(p1_sync)

            def p1_tensor(t):
                for s, (t0, W) in enumerate(P1_SLABS):
                    d_x1.wait(t, 16 * (s + 1))
                    dve_row.wait(t, s)
                    for c in range(NCH):
                        mm = t.matmul(ps_sum[:, 0:W], b.ones_f[:],
                                      x_sb[s % 2][:, c * W:(c + 1) * W],
                                      start=(c == 0), stop=(c == NCH - 1))
                    pe_sum.inc(mm)
                    for c in range(NCH):
                        act_sq.wait(t, 16 * s + c + 1)
                        mm = t.matmul(ps_sq[:, 0:W], b.ones_f[:],
                                      sq_sb[:, (c % 2) * 512:(c % 2) * 512 + W],
                                      start=(c == 0), stop=(c == NCH - 1))
                        pe_sq.inc(mm)
                    act_row.wait(t, 2 * (s + 1))
                    dve_stat.wait(t, s + 1)
                    rows = (b.rstd_all[0:1, t0:t0 + W], b.d2_all[0:1, t0:t0 + W],
                            r_rstd2[0:1, 0:W])
                    for j, row in enumerate(rows):
                        act_bc.wait(t, 3 * s + j - 1)
                        pe_bc.inc(t.matmul(ps_bc[j % 2][:, 0:W], b.onesrow_f[:],
                                           row, start=True, stop=True))
            b.blk.tensor(p1_tensor)

            def p1_scalar(a):
                for s, (t0, W) in enumerate(P1_SLABS):
                    d_x1.wait(a, 16 * (s + 1))
                    for c in range(NCH):
                        pe_sq.wait(a, 16 * s + c - 1)
                        act_sq.inc(a.activation(
                            sq_sb[:, (c % 2) * 512:(c % 2) * 512 + W],
                            x_sb[s % 2][:, c * W:(c + 1) * W], AF.Square))
                    dve_pre.wait(a, s + 1)
                    a.activation(r_tmp[0:1, 0:W], r_var[0:1, 0:W], AF.Ln,
                                 bias=b.eps_t[:])
                    act_row.inc(a.activation(b.rstd_all[0:1, t0:t0 + W],
                                             r_tmp[0:1, 0:W], AF.Exp,
                                             scale=b.neghalf_t[:]))
                    a.activation(r_tmp[0:1, 0:W], r_v2[0:1, 0:W], AF.Ln,
                                 bias=b.eps_t[:])
                    act_row.inc(a.activation(r_rstd2[0:1, 0:W], r_tmp[0:1, 0:W],
                                             AF.Exp, scale=b.neghalf_t[:]))
                    dve_chunk.wait(a, 16 * s)
                    for j in range(3):
                        pe_bc.wait(a, 3 * s + j + 1)
                        act_bc.inc(a.activation(bcast[:, j * 512:j * 512 + W],
                                                ps_bc[j % 2][:, 0:W], AF.Copy))
            b.blk.scalar(p1_scalar)

            def p1_vector(v):
                for s, (t0, W) in enumerate(P1_SLABS):
                    pe_sum.wait(v, s + 1)
                    pe_sq.wait(v, 16 * (s + 1))
                    act_row.wait(v, 2 * s)
                    v.tensor_scalar(r_mu[0:1, 0:W], ps_sum[0:1, 0:W],
                                    1.0 / D, None, ALU.mult)
                    dve_row.inc(v.tensor_scalar(r_msq[0:1, 0:W], ps_sq[0:1, 0:W],
                                                1.0 / D, None, ALU.mult))
                    v.tensor_mul(r_tmp[0:1, 0:W], r_mu[0:1, 0:W], r_mu[0:1, 0:W])
                    v.tensor_sub(r_var[0:1, 0:W], r_msq[0:1, 0:W], r_tmp[0:1, 0:W])
                    v.tensor_scalar(r_rec[0:1, 0:W], r_var[0:1, 0:W],
                                    EPS, None, ALU.add)
                    v.reciprocal(r_rec[0:1, 0:W], r_rec[0:1, 0:W])
                    dve_pre.inc(v.tensor_mul(r_v2[0:1, 0:W], r_var[0:1, 0:W],
                                             r_rec[0:1, 0:W]))
                    act_row.wait(v, 2 * (s + 1))
                    dve_stat.inc(v.scalar_tensor_tensor(
                        b.d2_all[0:1, t0:t0 + W], r_mu[0:1, 0:W], -1.0,
                        b.rstd_all[0:1, t0:t0 + W], ALU.mult, ALU.mult))
                    act_bc.wait(v, 3 * (s + 1))
                    d_xn.wait(v, 16 * s)
                    for c in range(NCH):
                        xc = x_sb[s % 2][:, c * W:(c + 1) * W]
                        v.tensor_mul(s1_sb[:, 0:W], xc, bcast[:, 0:W])
                        v.tensor_add(s2_sb[:, 0:W], s1_sb[:, 0:W],
                                     bcast[:, 512:512 + W])
                        v.tensor_copy(xn_st[:, c * W:(c + 1) * W], s2_sb[:, 0:W])
                        v.tensor_mul(s1_sb[:, 0:W], s2_sb[:, 0:W],
                                     bcast[:, 2 * 512:2 * 512 + W])
                        dve_chunk.inc(v.tensor_scalar(
                            b.xn2T[:, c * TKG + t0:c * TKG + t0 + W],
                            s1_sb[:, 0:W], b.gamma_sb[:, c:c + 1],
                            b.beta_sb[:, c:c + 1], ALU.mult, ALU.add))
            b.blk.vector(p1_vector)

            if b.stage == 1:
                d_dbg = b.sem(p1, "d_dbg")

                def p1_dbg(sy):
                    dve_chunk.wait(sy, 16 * len(P1_SLABS))
                    d_dbg.inc(sy.dma_start(
                        b.xn2sp, b.xn2T[:].rearrange("p (c t) -> p c t", t=TKG)), 16)
                    d_dbg.inc(sy.dma_start(b.rowsp[0:1, :], b.rstd_all[:]), 16)
                    d_dbg.inc(sy.dma_start(b.rowsp[1:2, :], b.d2_all[:]), 16)
                    sy.wait_ge(d_dbg.h, d_dbg.n)
                b.blk.sync(p1_dbg)

    def emit_p2(self, p2):
        b = self
        b.wq_sl = b.sbt(p2, "wq_sl", [128, NCH * 256], BF16)
        b.wk_sl = b.sbt(p2, "wk_sl", [128, NCH * 256], BF16)
        b.wv_sl = b.sbt(p2, "wv_sl", [128, NCH * 256], BF16)
        Q_sb = [b.sbt(p2, "Q_sb0", [128, 2 * TOWN], BF16),
                b.sbt(p2, "Q_sb1", [128, 2 * TOWN], BF16)]
        K_sb = [b.sbt(p2, "K_sb0", [128, 2 * TKG], BF16),
                b.sbt(p2, "K_sb1", [128, 2 * TKG], BF16)]
        V_sb = [b.sbt(p2, "V_sb0", [128, 19 * 256], BF16),
                b.sbt(p2, "V_sb1", [128, 19 * 256], BF16)]
        O_sb = [b.sbt(p2, "O_sb0", [128, TOWN], BF16),
                b.sbt(p2, "O_sb1", [128, TOWN], BF16)]
        Z_sb = [b.sbt(p2, "Z_sb0", [128, 400]), b.sbt(p2, "Z_sb1", [128, 400])]
        E_sb = [b.sbt(p2, "E_sb0", [128, 400], BF16),
                b.sbt(p2, "E_sb1", [128, 400], BF16)]
        PT_sb = [b.sbt(p2, "PT_sb0", [128, 512], BF16),
                 b.sbt(p2, "PT_sb1", [128, 512], BF16)]
        dg_sb = [b.sbt(p2, "dg_sb0", [128, 128], BF16),
                 b.sbt(p2, "dg_sb1", [128, 128], BF16)]
        nm_sb = [b.sbt(p2, "nm_sb0", [128, 1]), b.sbt(p2, "nm_sb1", [128, 1])]
        l_sb = [b.sbt(p2, "l_sb0", [128, 1]), b.sbt(p2, "l_sb1", [128, 1])]
        li_sb = [b.sbt(p2, "li_sb0", [128, 1]), b.sbt(p2, "li_sb1", [128, 1])]
        ps_pr = [b.pst(p2, "ps_pr0", [128, 512]), b.pst(p2, "ps_pr1", [128, 512])]
        ps_S = [b.pst(p2, "ps_S0", [128, 512]), b.pst(p2, "ps_S1", [128, 512])]
        ps_PT = [b.pst(p2, "ps_PT0", [128, 512]), b.pst(p2, "ps_PT1", [128, 512])]
        ps_O = [b.pst(p2, "ps_O0", [128, 512]), b.pst(p2, "ps_O1", [128, 512])]

        pe_prod = b.sem(p2, "pe_prod")
        act_prod = b.sem(p2, "act_prod")
        pe_s = b.sem(p2, "pe_s")
        pe_t = b.sem(p2, "pe_t")
        pe_av = b.sem(p2, "pe_av")
        dve_z = b.sem(p2, "dve_z")
        dve_l = b.sem(p2, "dve_l")
        dve_p = b.sem(p2, "dve_p")
        act_e = b.sem(p2, "act_e")
        act_pt = b.sem(p2, "act_pt")
        act_o = b.sem(p2, "act_o")
        NPR = 37

        def prod_due(i):
            return min(NPR, ((i + 1) * NPR + NIT + 1) // (NIT + 2))

        NBG = 96
        NBGG = NBG // NG
        bg_st = [b.sbt(p2, "bg_st0", [128, 4096], BF16),
                 b.sbt(p2, "bg_st1", [128, 4096], BF16)]
        d_bgin = b.sem(p2, "d_bgin")
        p_bg = b.sem(p2, "p_bg")

        def bg_srcdst(k):
            if k < 64:
                ib, r = k // 4, k % 4
                j = 2 * ib + (r % 2)
                src, dst = (b.gate, b.gbf) if r < 2 else (b.upw, b.ubf)
                return (src[:, :, j * 256:(j + 1) * 256],
                        dst[:, :, j * 256:(j + 1) * 256], 256)
            kk = k - 64
            hc, h = kk // 2, kk % 2
            return (b.down[hc, :, h * 32:(h + 1) * 32, :],
                    b.dbf[hc, :, h * 32:(h + 1) * 32, :], 128)

        def bg_issue_in(g, k):
            src, _, inner = bg_srcdst(k)
            d_bgin.inc(g.dma_start(
                bg_st[k % 2][:].rearrange("p (a i) -> p a i", i=inner), src), 16)

        def bg_step(g, k):
            _, dst, inner = bg_srcdst(k)
            p_bg.wait(g, k + 1)
            b.bg.inc(g.dma_start(
                dst, bg_st[k % 2][:].rearrange("p (a i) -> p a i", i=inner)), 16)
            if k + 2 < NBG:
                g.wait_ge(b.bg.h, b.bg.n)
                bg_issue_in(g, k + 2)

        def slice_dma(g, gi, which):
            w, sl = ((b.wq, b.wq_sl), (b.wk, b.wk_sl), (b.wv, b.wv_sl))[which]
            b.wl.inc(g.dma_start(
                sl[:].rearrange("p (c o) -> p c o", o=256),
                w[:, :, gi * 256:(gi + 1) * 256]), 16)

        def p2_gpsimd(g):
            for which in range(3):
                slice_dma(g, 0, which)
            bg_issue_in(g, 0)
            bg_issue_in(g, 1)
            for which, gate in ((0, 8), (1, 18), (2, 37)):
                pe_prod.wait(g, gate)
                slice_dma(g, 1, which)
            for gi in range(NG):
                for idx, k in enumerate(range(NBGG * gi, NBGG * (gi + 1))):
                    bg_step(g, k)
                    if gi + 2 < NG:
                        if idx == 4:
                            pe_prod.wait(g, NPR * (gi + 1) + 8)
                            slice_dma(g, gi + 2, 0)
                        elif idx == 6:
                            pe_prod.wait(g, NPR * (gi + 1) + 18)
                            slice_dma(g, gi + 2, 1)
                if gi + 2 < NG:
                    pe_prod.wait(g, NPR * (gi + 2))
                    slice_dma(g, gi + 2, 2)
            g.wait_ge(b.bg.h, b.bg.n)
        b.blk.gpsimd(p2_gpsimd)

        def prod_pairs(gi):
            out = []
            e = gi % 2
            for hh in range(2):
                for sl in range(4):
                    pairs = [(b.wq_sl[:, c * 256 + hh * 128:c * 256 + (hh + 1) * 128],
                              b.xn2T[:, c * TKG + HALO + sl * 512:c * TKG + HALO + (sl + 1) * 512])
                             for c in range(NCH)]
                    out.append((512, pairs,
                                Q_sb[e][:, hh * TOWN + sl * 512:hh * TOWN + (sl + 1) * 512],
                                b.qscale_t))
            kslabs = [(0, 512), (512, 512), (1024, 512), (1536, 512), (2048, 272)]
            for hh in range(2):
                for (t0, W) in kslabs:
                    pairs = [(b.wk_sl[:, c * 256 + hh * 128:c * 256 + (hh + 1) * 128],
                              b.xn2T[:, c * TKG + t0:c * TKG + t0 + W])
                             for c in range(NCH)]
                    out.append((W, pairs,
                                K_sb[e][:, hh * TKG + t0:hh * TKG + t0 + W], None))
            for kt in range(18):
                pairs = [(b.xn2T[:, c * TKG + kt * 128:c * TKG + (kt + 1) * 128],
                          b.wv_sl[:, c * 256:(c + 1) * 256])
                         for c in range(NCH)]
                out.append((256, pairs, V_sb[e][:, kt * 256:(kt + 1) * 256], None))
            pairs = [(b.xn2T[:, c * TKG + TK:c * TKG + TKG],
                      b.wv_sl[:, c * 256:(c + 1) * 256])
                     for c in range(NCH)]
            out.append((256, pairs, V_sb[e][0:16, 18 * 256:19 * 256], None))
            return out

        def emit_prod(t, gi, j):
            gidx = NPR * gi + j
            if j == 0:
                b.wl.wait(t, 48 * gi + 16)
            elif j == 8:
                b.wl.wait(t, 48 * gi + 32)
            elif j == 18:
                b.wl.wait(t, 48 * gi + 48)
            act_prod.wait(t, gidx if j == 0 else gidx - 1)
            W, pairs, dst, scale = prod_groups[gi][j]
            for k, (lh, rh) in enumerate(pairs):
                mm = t.matmul(ps_pr[j % 2][0:dst.shape[0], 0:W], lh, rh,
                              start=(k == 0), stop=(k == len(pairs) - 1))
            pe_prod.inc(mm)

        def emit_evac(a, gi, j):
            gidx = NPR * gi + j
            pe_prod.wait(a, gidx + 1)
            W, pairs, dst, scale = prod_groups[gi][j]
            rows = dst.shape[0]
            if scale is not None:
                act_prod.inc(a.activation(dst, ps_pr[j % 2][0:rows, 0:W],
                                          AF.Copy, scale=scale[0:rows, :]))
            else:
                act_prod.inc(a.activation(dst, ps_pr[j % 2][0:rows, 0:W],
                                          AF.Copy))

        prod_groups = [prod_pairs(gi) for gi in range(NG)]

        def p2_tensor(t):
            for j in range(NPR):
                emit_prod(t, 0, j)
            for gi in range(NG):
                base = NIT * gi
                emitted = 0
                has_next = gi + 1 < NG
                for i in range(NIT + 2):
                    it = base + i
                    qt, hh = i // 2, i % 2
                    if i == 0:
                        act_prod.wait(t, NPR * gi + 18)
                    if i == 2:
                        act_prod.wait(t, NPR * (gi + 1))
                    if i < NIT:
                        dve_z.wait(t, it - 1)
                        lq = Q_sb[gi % 2][:, hh * TOWN + qt * 128:hh * TOWN + (qt + 1) * 128]
                        t.matmul(ps_S[it % 2][:, 0:384], lq,
                                 K_sb[gi % 2][:, hh * TKG + qt * 128:hh * TKG + qt * 128 + 384],
                                 start=True, stop=True)
                        pe_s.inc(t.matmul(ps_S[it % 2][:, 384:400], lq,
                                          K_sb[gi % 2][:, hh * TKG + TK:hh * TKG + TKG],
                                          start=True, stop=True))
                    if 0 <= i - 1 < NIT:
                        j = it - 1
                        act_e.wait(t, j + 1)
                        dve_p.wait(t, j + 1)
                        act_pt.wait(t, j - 1)
                        for k in range(3):
                            t.matmul(ps_PT[j % 2][:, k * 128:(k + 1) * 128],
                                     E_sb[j % 2][:, k * 128:(k + 1) * 128],
                                     dg_sb[j % 2][:], start=True, stop=True)
                        pe_t.inc(t.matmul(ps_PT[j % 2][0:16, 384:512],
                                          E_sb[j % 2][:, 384:400],
                                          dg_sb[j % 2][:], start=True, stop=True))
                    if 0 <= i - 2 < NIT:
                        j = it - 2
                        jq, jh = (i - 2) // 2, (i - 2) % 2
                        act_pt.wait(t, j + 1)
                        act_o.wait(t, j - 1)
                        for k in range(3):
                            kt = jq + k
                            t.matmul(ps_O[j % 2][:, 0:128],
                                     V_sb[gi % 2][:, kt * 256 + jh * 128:kt * 256 + (jh + 1) * 128],
                                     PT_sb[j % 2][:, k * 128:(k + 1) * 128],
                                     start=(k == 0), stop=False)
                        pe_av.inc(t.matmul(
                            ps_O[j % 2][:, 0:128],
                            V_sb[gi % 2][0:16, 18 * 256 + jh * 128:18 * 256 + (jh + 1) * 128],
                            PT_sb[j % 2][0:16, 384:512],
                            start=False, stop=True))
                    if has_next and (i % 2 == 1 or i >= NIT):
                        while emitted < prod_due(i):
                            emit_prod(t, gi + 1, emitted)
                            emitted += 1
        b.blk.tensor(p2_tensor)

        def p2_scalar(a):
            for j in range(NPR):
                emit_evac(a, 0, j)
            for gi in range(NG):
                base = NIT * gi
                emitted = 0
                has_next = gi + 1 < NG
                for i in range(NIT + 2):
                    it = base + i
                    if i < NIT:
                        dve_z.wait(a, it + 1)
                        pe_t.wait(a, it - 1)
                        dve_l.wait(a, it - 1)
                        act_e.inc(a.activation(E_sb[it % 2][:], Z_sb[it % 2][:],
                                               AF.Exp, bias=nm_sb[it % 2][:],
                                               accum_out=l_sb[it % 2][:]))
                    if 0 <= i - 1 < NIT:
                        j = it - 1
                        pe_t.wait(a, j + 1)
                        pe_av.wait(a, j - 1)
                        a.activation(PT_sb[j % 2][:, 0:384], ps_PT[j % 2][:, 0:384],
                                     AF.Copy)
                        act_pt.inc(a.activation(PT_sb[j % 2][0:16, 384:512],
                                                ps_PT[j % 2][0:16, 384:512], AF.Copy))
                    if 0 <= i - 2 < NIT:
                        j = it - 2
                        jq, jh = (i - 2) // 2, (i - 2) % 2
                        pe_av.wait(a, j + 1)
                        if jq == 0:
                            b.st_o[jh].wait(a, 16 * gi)
                        act_o.inc(a.activation(
                            O_sb[jh][:, jq * 128:(jq + 1) * 128],
                            ps_O[j % 2][:, 0:128], AF.Copy))
                    if has_next:
                        while emitted < prod_due(i - 1):
                            emit_evac(a, gi + 1, emitted)
                            emitted += 1
                if has_next:
                    while emitted < NPR:
                        emit_evac(a, gi + 1, emitted)
                        emitted += 1
        b.blk.scalar(p2_scalar)

        def p2_vector(v):
            bgk = [0]

            def bg_sub(upto):
                while bgk[0] < min(upto, NBG):
                    k = bgk[0]
                    d_bgin.wait(v, 16 * (k + 1))
                    p_bg.inc(v.tensor_scalar_add(bg_st[k % 2][:],
                                                 bg_st[k % 2][:], -128.0))
                    bgk[0] += 1

            for gi in range(NG):
                base = NIT * gi
                for i in range(NIT + 1):
                    it = base + i
                    qt = i // 2
                    if 0 <= i - 1 < NIT:
                        j = it - 1
                        act_e.wait(v, j + 1)
                        pe_t.wait(v, j - 1)
                        dve_l.inc(v.reciprocal(li_sb[j % 2][:], l_sb[j % 2][:]))
                        dve_l.wait(v, dve_l.n)
                        dve_p.inc(v.tensor_scalar(dg_sb[j % 2][:], b.ident[:],
                                                  li_sb[j % 2][:], None, ALU.mult))
                    if i < NIT:
                        pe_s.wait(v, it + 1)
                        act_e.wait(v, it - 1)
                        mi = min(qt, 2)
                        v.tensor_add(Z_sb[it % 2][:], ps_S[it % 2][:, 0:400],
                                     b.mask_sb[:, mi * 400:(mi + 1) * 400])
                        dve_z.inc(v.reduce_max(nm_sb[it % 2][:], Z_sb[it % 2][:],
                                               axis=AX.X, negate=True))
                    bg_sub(NBGG * gi + ((i + 1) * NBGG) // (NIT + 1))
        b.blk.vector(p2_vector)

        def p2_sync(sy):
            for gi in range(NG):
                for hh in range(2):
                    head = gi * 2 + hh
                    act_o.wait(sy, NIT * gi + 31 + hh)
                    b.st_o[hh].inc(sy.dma_start(b.osp[:, head, :], O_sb[hh][:]), 16)
            sy.wait_ge(b.st_o[0].h, b.st_o[0].n)
            sy.wait_ge(b.st_o[1].h, b.st_o[1].n)
        b.blk.sync(p2_sync)


    def emit_p2c(self, p2c):
        b = self
        O_sl = [b.sbt(p2c, "O_sl0", [128, NH * 512], BF16),
                b.sbt(p2c, "O_sl1", [128, NH * 512], BF16)]
        xn_sl = [b.sbt(p2c, "xn_sl0", [128, NCH * 512], BF16),
                 b.sbt(p2c, "xn_sl1", [128, NCH * 512], BF16)]
        wo_all = b.sbt(p2c, "wo_all", [128, NCH * D], BF16)
        s_sb = [b.sbt(p2c, "s_sb0", [128, 512]), b.sbt(p2c, "s_sb1", [128, 512])]
        ps_s2 = [b.pst(p2c, "ps_s20", [128, 512]), b.pst(p2c, "ps_s21", [128, 512])]

        d_wo = b.sem(p2c, "d_wo")
        d_ox = [b.sem(p2c, "d_ox0"), b.sem(p2c, "d_ox1")]
        st_s2 = [b.sem(p2c, "st_s20"), b.sem(p2c, "st_s21")]
        pe_g2 = b.sem(p2c, "pe_g2")
        dve_s2 = b.sem(p2c, "dve_s2")

        def p2c_gpsimd(g):
            for j in range(8):
                if j >= 2:
                    g.wait_ge(d_wo.h, 16 * (j - 1))
                d_wo.inc(g.dma_start(
                    wo_all[:].rearrange("p (c o) -> p c o", o=D)
                    [:, :, j * 256:(j + 1) * 256],
                    b.wo[:, :, j * 256:(j + 1) * 256]), 16)
        b.blk.gpsimd(p2c_gpsimd)

        def load_ox(sy, s):
            d_ox[s % 2].inc(sy.dma_start(
                O_sl[s % 2][:].rearrange("p (h t) -> p h t", t=512),
                b.osp[:, :, s * 512:(s + 1) * 512]), 16)
            d_ox[s % 2].inc(sy.dma_start(
                xn_sl[s % 2][:].rearrange("p (c t) -> p c t", t=512),
                b.xnsp[:, :, s * 512:(s + 1) * 512]), 16)

        def p2c_sync(sy):
            load_ox(sy, 0)
            load_ox(sy, 1)
            for s in range(4):
                for oc in range(16):
                    m = s * 16 + oc
                    dve_s2.wait(sy, m + 1)
                    st_s2[m % 2].inc(sy.dma_start(
                        b.ssp[:, oc, s * 512:(s + 1) * 512], s_sb[m % 2][:]), 16)
                if s + 2 < 4:
                    load_ox(sy, s + 2)
            sy.wait_ge(st_s2[0].h, st_s2[0].n)
            sy.wait_ge(st_s2[1].h, st_s2[1].n)
        b.blk.sync(p2c_sync)

        def p2c_tensor(t):
            for s in range(4):
                d_ox[s % 2].wait(t, 32 * (s // 2 + 1))
                for oc in range(16):
                    m = s * 16 + oc
                    d_wo.wait(t, 16 * min(8, oc // 2 + 3))
                    dve_s2.wait(t, m - 1)
                    for c in range(NCH):
                        mm = t.matmul(ps_s2[m % 2][:, 0:512],
                                      wo_all[:, c * D + oc * 128:c * D + (oc + 1) * 128],
                                      O_sl[s % 2][:, c * 512:(c + 1) * 512],
                                      start=(c == 0), stop=(c == NCH - 1))
                    pe_g2.inc(mm)
        b.blk.tensor(p2c_tensor)

        def p2c_vector(v):
            for s in range(4):
                d_ox[s % 2].wait(v, 32 * (s // 2 + 1))
                for oc in range(16):
                    m = s * 16 + oc
                    pe_g2.wait(v, m + 1)
                    st_s2[m % 2].wait(v, 16 * (m // 2))
                    dve_s2.inc(v.tensor_add(
                        s_sb[m % 2][:], ps_s2[m % 2][:, 0:512],
                        xn_sl[s % 2][:, oc * 512:(oc + 1) * 512]))
        b.blk.vector(p2c_vector)


    def emit_p3(self, p3):
        b = self
        hT = b.sbt(p3, "hT", [128, NIC * 512], BF16)
        xns = b.sbt(p3, "xns", [128, NCH * 512], BF16)
        wbuf = b.sbt(p3, "wbuf", [128, 4 * NCH * 512], BF16)
        sg_sb = [b.sbt(p3, "sg_sb0", [128, 512]), b.sbt(p3, "sg_sb1", [128, 512])]
        uu_sb = [b.sbt(p3, "uu_sb0", [128, 512]), b.sbt(p3, "uu_sb1", [128, 512])]
        gf_sb = [b.sbt(p3, "gf_sb0", [128, 512]), b.sbt(p3, "gf_sb1", [128, 512])]
        x_c = [b.sbt(p3, "x_c0", [128, 512]), b.sbt(p3, "x_c1", [128, 512])]
        s_c = [b.sbt(p3, "s_c0", [128, 512]), b.sbt(p3, "s_c1", [128, 512])]
        o_c = [b.sbt(p3, "o_c0", [128, 512]), b.sbt(p3, "o_c1", [128, 512])]
        t3_sb = b.sbt(p3, "t3_sb", [128, 512])
        ps_g = [b.pst(p3, "ps_g0", [128, 512]), b.pst(p3, "ps_g1", [128, 512])]
        ps_u = [b.pst(p3, "ps_u0", [128, 512]), b.pst(p3, "ps_u1", [128, 512])]
        ps_f = [b.pst(p3, "ps_f0", [128, 512]), b.pst(p3, "ps_f1", [128, 512])]

        d_xns = b.sem(p3, "d_xns")
        d_wgu = b.sem(p3, "d_wgu")
        d_wd = b.sem(p3, "d_wd")
        d_xs = [b.sem(p3, "d_xs0"), b.sem(p3, "d_xs1")]
        st_out = [b.sem(p3, "st_out0"), b.sem(p3, "st_out1")]
        pe_gu = b.sem(p3, "pe_gu")
        act_gu = b.sem(p3, "act_gu")
        dve_h = b.sem(p3, "dve_h")
        pe_dn = b.sem(p3, "pe_dn")
        act_dn = b.sem(p3, "act_dn")
        dve_o = b.sem(p3, "dve_o")
        self.st_out = st_out

        def p3_sync(sy):
            for s in range(4):
                pe_gu.wait(sy, 128 * s)
                d_xns.inc(sy.dma_start(
                    xns[:].rearrange("p (c t) -> p c t", t=512),
                    b.xnsp[:, :, s * 512:(s + 1) * 512]), 16)
                for ib in range(16):
                    q = (ib % 2) * 2
                    pe_gu.wait(sy, 128 * s + 8 * (ib - 1))
                    if ib <= 1:
                        pe_dn.wait(sy, 16 * s)
                    b.bg.wait(sy, 16 * (4 * ib + 2))
                    d_wgu.inc(sy.dma_start(
                        wbuf[:, q * NCH * 512:(q + 1) * NCH * 512]
                        .rearrange("p (c i) -> p c i", i=512),
                        b.gbf[:, :, ib * 512:(ib + 1) * 512]), 16)
                    b.bg.wait(sy, 16 * (4 * ib + 4))
                    d_wgu.inc(sy.dma_start(
                        wbuf[:, (q + 1) * NCH * 512:(q + 2) * NCH * 512]
                        .rearrange("p (c i) -> p c i", i=512),
                        b.ubf[:, :, ib * 512:(ib + 1) * 512]), 16)
                for hc in range(16):
                    m = 16 * s + hc
                    pe_gu.wait(sy, 128 * (s + 1))
                    pe_dn.wait(sy, m - 1)
                    b.bg.wait(sy, 16 * (64 + 2 * hc + 2))
                    d_wd.inc(sy.dma_start(
                        wbuf[:, (hc % 2) * NIC * 128:((hc % 2) + 1) * NIC * 128]
                        .rearrange("p (ci h) -> p ci h", h=128),
                        b.dbf[hc]), 16)
        b.blk.sync(p3_sync)

        def p3_gpsimd(g):
            for s in range(4):
                for hc in range(16):
                    m = 16 * s + hc
                    dve_o.wait(g, m - 1)
                    d_xs[hc % 2].inc(g.dma_start(
                        x_c[hc % 2][:],
                        b.xT[:, hc, HALO + s * 512:HALO + (s + 1) * 512]), 16)
                    d_xs[hc % 2].inc(g.dma_start(
                        s_c[hc % 2][:],
                        b.ssp[:, hc, s * 512:(s + 1) * 512]), 16)
                    dve_o.wait(g, m + 1)
                    st_out[hc % 2].inc(g.dma_start(
                        b.outT[:, hc, s * 512:(s + 1) * 512], o_c[hc % 2][:]), 16)
        b.blk.gpsimd(p3_gpsimd)

        def p3_tensor(t):
            for s in range(4):
                d_xns.wait(t, 16 * (s + 1))
                for ib in range(16):
                    q = (ib % 2) * 2
                    for ic in range(4):
                        gq = 128 * s + 8 * ib + 2 * ic
                        d_wgu.wait(t, 16 * (2 * (16 * s + ib) + 1))
                        act_gu.wait(t, gq - 2)
                        for c in range(NCH):
                            mm = t.matmul(
                                ps_g[ic % 2][:, 0:512],
                                wbuf[:, q * NCH * 512 + c * 512 + ic * 128:
                                     q * NCH * 512 + c * 512 + (ic + 1) * 128],
                                xns[:, c * 512:(c + 1) * 512],
                                start=(c == 0), stop=(c == NCH - 1))
                        pe_gu.inc(mm)
                        d_wgu.wait(t, 16 * (2 * (16 * s + ib) + 2))
                        act_gu.wait(t, gq - 1)
                        for c in range(NCH):
                            mm = t.matmul(
                                ps_u[ic % 2][:, 0:512],
                                wbuf[:, (q + 1) * NCH * 512 + c * 512 + ic * 128:
                                     (q + 1) * NCH * 512 + c * 512 + (ic + 1) * 128],
                                xns[:, c * 512:(c + 1) * 512],
                                start=(c == 0), stop=(c == NCH - 1))
                        pe_gu.inc(mm)
                dve_h.wait(t, 64 * (s + 1))
                for hc in range(16):
                    m = 16 * s + hc
                    d_wd.wait(t, 16 * (m + 1))
                    act_dn.wait(t, m - 1)
                    for ci in range(NIC):
                        mm = t.matmul(ps_f[hc % 2][:, 0:512],
                                      wbuf[:, (hc % 2) * NIC * 128 + ci * 128:
                                           (hc % 2) * NIC * 128 + (ci + 1) * 128],
                                      hT[:, ci * 512:(ci + 1) * 512],
                                      start=(ci == 0), stop=(ci == NIC - 1))
                    pe_dn.inc(mm)
        b.blk.tensor(p3_tensor)

        def p3_scalar(a):
            for s in range(4):
                for ib in range(16):
                    for ic in range(4):
                        gq = 128 * s + 8 * ib + 2 * ic
                        icg = ib * 4 + ic
                        pe_gu.wait(a, gq + 1)
                        dve_h.wait(a, (gq // 2) - 1)
                        act_gu.inc(a.activation(sg_sb[icg % 2][:],
                                                ps_g[ic % 2][:, 0:512], AF.Silu,
                                                scale=b.gsc_sb[:, icg:icg + 1]))
                        pe_gu.wait(a, gq + 2)
                        act_gu.inc(a.activation(uu_sb[icg % 2][:],
                                                ps_u[ic % 2][:, 0:512], AF.Copy,
                                                scale=b.usc_sb[:, icg:icg + 1]))
                for hc in range(16):
                    m = 16 * s + hc
                    pe_dn.wait(a, m + 1)
                    dve_o.wait(a, m - 1)
                    act_dn.inc(a.activation(gf_sb[hc % 2][:],
                                            ps_f[hc % 2][:, 0:512], AF.Silu,
                                            scale=b.dsc_sb[:, hc:hc + 1]))
        b.blk.scalar(p3_scalar)

        def p3_vector(v):
            for s in range(4):
                pe_dn.wait(v, 16 * s)
                for ib in range(16):
                    for ic in range(4):
                        icg = ib * 4 + ic
                        gq = 128 * s + 8 * ib + 2 * ic
                        act_gu.wait(v, gq + 2)
                        dve_h.inc(v.tensor_mul(
                            hT[:, icg * 512:(icg + 1) * 512],
                            sg_sb[icg % 2][:], uu_sb[icg % 2][:]))
                for hc in range(16):
                    m = 16 * s + hc
                    act_dn.wait(v, m + 1)
                    d_xs[m % 2].wait(v, 32 * (m // 2 + 1))
                    st_out[m % 2].wait(v, 16 * (m // 2))
                    v.tensor_mul(t3_sb[:], s_c[hc % 2][:], gf_sb[hc % 2][:])
                    dve_o.inc(v.tensor_add(o_c[hc % 2][:], x_c[hc % 2][:],
                                           t3_sb[:]))
        b.blk.vector(p3_vector)

    def emit_final(self):
        b = self

        def fin(sy):
            if b.stage >= 4:
                sy.wait_ge(b.st_out[0].h, b.st_out[0].n)
                sy.wait_ge(b.st_out[1].h, b.st_out[1].n)
            elif b.stage >= 2:
                sy.wait_ge(b.st_o[0].h, b.st_o[0].n)
                sy.wait_ge(b.st_o[1].h, b.st_o[1].n)
        b.blk.sync(fin)



def _shared_inputs(inputs):
    f32 = np.float32
    Wq = np.asarray(inputs["L0_attn_Wq"], f32)
    Wk = np.asarray(inputs["L0_attn_Wk"], f32)
    Wv = np.asarray(inputs["L0_attn_Wv"], f32)
    Wo = np.asarray(inputs["L0_attn_Wo"], f32)
    gw = np.asarray(inputs["L0_ffn_gate_w"], np.int32)
    uw = np.asarray(inputs["L0_ffn_up_w"], np.int32)
    dw = np.asarray(inputs["L0_ffn_down_w"], np.int32)

    def pmaj(wt):
        return np.ascontiguousarray(wt.reshape(NCH, 128, wt.shape[1]).transpose(1, 0, 2))

    sh = {
        "wq": pmaj(Wq.T), "wk": pmaj(Wk.T), "wv": pmaj(Wv.T), "wo": pmaj(Wo.T),
        "gate": pmaj(gw.T), "upw": pmaj(uw.T),
        "down": np.ascontiguousarray(
            dw.T.reshape(NIC, 128, NCH, 128).transpose(2, 1, 0, 3)),
        "gT": np.ascontiguousarray(
            np.asarray(inputs["L0_attn_global_tokens"], f32)[0].T
            .reshape(NCH, 128, G).transpose(1, 0, 2)),
        "gamma": np.ascontiguousarray(
            np.asarray(inputs["L0_attn_ln_scale"], f32).reshape(NCH, 128).T),
        "beta": np.ascontiguousarray(
            np.asarray(inputs["L0_attn_ln_bias"], f32).reshape(NCH, 128).T),
        "gsc": np.ascontiguousarray(
            np.asarray(inputs["L0_ffn_gate_scale"], f32).reshape(NIC, 128).T),
        "usc": np.ascontiguousarray(
            np.asarray(inputs["L0_ffn_up_scale"], f32).reshape(NIC, 128).T),
        "dsc": np.ascontiguousarray(
            np.asarray(inputs["L0_ffn_down_scale"], f32).reshape(NCH, 128).T),
        "identin": np.eye(128, dtype=ml_dtypes.bfloat16),
    }
    return sh


def _masks_for(half):
    r = np.arange(128)[:, None]
    cc = np.arange(384)[None, :]
    band = (cc >= r + 1) & (cc <= r + 256)
    out = np.full((128, 3, 400), NEG, np.float32)
    out[:, :, 384:400] = 0.0
    for mi in range(3):
        allowed = band.copy()
        if half == 0 and mi == 0:
            allowed &= cc >= 256
        elif half == 0 and mi == 1:
            allowed &= cc >= 128
        out[:, mi, 0:384] = np.where(allowed, 0.0, NEG)
    return out


def _core_x(x, b, half):
    xb = x[b]
    seg = np.zeros((TK, D), np.float32)
    seg[HALO:] = xb[half * TOWN:(half + 1) * TOWN]
    if half == 1:
        seg[:HALO] = xb[TOWN - HALO:TOWN]
    return np.ascontiguousarray(seg.T.reshape(NCH, 128, TK).transpose(1, 0, 2))


_NC_CACHE = {}


def _get_nc(stage=4):
    if stage not in _NC_CACHE:
        _NC_CACHE[stage] = Builder(stage).build()
    return _NC_CACHE[stage]


def kernel(**inputs):
    x = np.asarray(inputs["L0_attn_x"], np.float32)
    sh = _shared_inputs(inputs)
    masks = {h: _masks_for(h) for h in (0, 1)}
    in_maps = []
    for core in range(8):
        b, half = core // 2, core % 2
        m = dict(sh)
        m["xT"] = _core_x(x, b, half)
        m["masks"] = masks[half]
        in_maps.append(m)
    nc = _get_nc(4)
    res = run_bass_kernel_spmd(nc, in_maps, core_ids=list(range(8)))
    out = np.empty((4, 4096, 2048), np.float32)
    for core in range(8):
        b, half = core // 2, core % 2
        o = np.asarray(res.results[core]["outT"])
        out[b, half * TOWN:(half + 1) * TOWN, :] = (
            o.transpose(2, 1, 0).reshape(TOWN, D))
    return out
